# version 25
# baseline (speedup 1.0000x reference)
import os
import numpy as np
from contextlib import ExitStack

import concourse.bass as bass
import concourse.mybir as mybir
from concourse.bass_utils import run_bass_kernel_spmd

F32 = mybir.dt.float32
BF16 = mybir.dt.bfloat16
AF = mybir.ActivationFunctionType
ALU = mybir.AluOpType
AX = mybir.AxisListType

D = 2048
T = 2048
TC = 512
NCH = T // TC
DI = 512
NCT = DI // 128
NKD = D // 128
NPAR = 9
NPAR_AUX = 5
FH = 5632
NFC = FH // 128
TH = 1024
FWIN = 128
NWIN = FH // FWIN
EPS_LN = 1e-5
EPS_Q = 1e-8
NW_ELEM = float(FH * D)

ENGINES = ("pe", "dve", "act", "pool", "sp")


class Sched:
    def __init__(self, nc):
        self.nc = nc
        self.streams = {e: [] for e in ENGINES}
        self.counts = {}
        self.handles = {}

    def sem(self, ctx, name):
        h = ctx.enter_context(self.nc.semaphore(name))
        self.counts[name] = 0
        self.handles[name] = h

    def c(self, name):
        return self.counts[name]

    def op(self, eng, fn, waits=(), inc=None, amt=1, relax=1):
        w = [(self.handles[s], v) for (s, v) in waits if v > 0]
        if not hasattr(self, "debug"):
            self.debug = []
        self.debug.append((eng, [(s, v) for (s, v) in waits if v > 0], inc, amt))
        if inc is not None:
            self.counts[inc] += amt
            self.streams[eng].append((w, fn, self.handles[inc], amt, relax))
        else:
            self.streams[eng].append((w, fn, None, 0, relax))

    def dma(self, queue, out, in_, waits=(), inc=None):
        self.op(queue, (lambda o, i: (lambda e: e.dma_start(o, i)))(out, in_),
                waits=waits, inc=inc, amt=16)

    def dmat(self, queue, out, in_, waits=(), inc=None):
        self.op(queue,
                (lambda o, i: (lambda e: e.dma_start_transpose(o, i)))(out, in_),
                waits=waits, inc=inc, amt=16)

    def emit(self, block, self_sems):
        nc = self.nc
        eng_obj = {"pe": nc.tensor, "dve": nc.vector, "act": nc.scalar,
                   "pool": nc.gpsimd, "sp": nc.sync}

        class SerialProxy:
            _SKIP = {"wait_ge", "dma_start", "dma_start_transpose", "nop",
                     "engine_nop", "sem_inc", "sem_clear"}

            def __init__(self, eng, sem):
                self._eng = eng
                self._sem = sem
                self._n = 0
                self._relax = 1
                self._base = 0

            def __getattr__(self, name):
                real = getattr(self._eng, name)
                if name in self._SKIP or not callable(real):
                    return real

                def wrap(*a, **kw):
                    tgt = max(self._base, self._n - (self._relax - 1))
                    if tgt > 0:
                        self._eng.wait_ge(self._sem, tgt)
                    instr = real(*a, **kw)
                    if instr is not None:
                        instr.then_inc(self._sem, 1)
                        self._n += 1
                    return instr
                return wrap

        def mk(ename):
            def body(_):
                raw = eng_obj[ename]
                e = raw
                prox = None
                if ename in self_sems:
                    prox = SerialProxy(raw, self_sems[ename])
                    e = prox
                for (waits, fn, inc_h, amt, relax) in self.streams[ename]:
                    for (h, v) in waits:
                        raw.wait_ge(h, v)
                    n0 = prox._n if prox else 0
                    if prox is not None:
                        prox._relax = relax
                        prox._base = prox._n
                    instr = fn(e)
                    if prox is not None:
                        prox._relax = 1
                    if inc_h is not None:
                        assert instr is not None, f"{ename} op with inc returned None"
                        if prox is not None and prox._n > n0:
                            raw.wait_ge(self_sems[ename], prox._n)
                            raw.sem_inc(inc_h, amt)
                        else:
                            instr.then_inc(inc_h, amt)
            return body

        block.tensor(mk("pe"))
        block.vector(mk("dve"))
        block.scalar(mk("act"))
        block.gpsimd(mk("pool"))
        block.sync(mk("sp"))


def build_program():
    STAGE = int(os.environ.get("KSTAGE", "4"))
    nc = bass.Bass("TRN2", target_bir_lowering=False)

    def din(name, shape, dtype=F32):
        return nc.dram_tensor(name, shape, dtype, kind="ExternalInput").ap()

    hs_tm = din("hs_tm", [T, D])
    hs_cm = din("hs_cm", [D, T])
    aux_cm = din("aux_cm", [D, T])
    winxT = din("winxT", [D, DI])
    winzT = din("winzT", [D, DI])
    wparamT = din("wparamT", [DI, 1152])
    wdtT = din("wdtT", [128, DI])
    woutT = din("woutT", [DI, D])
    w1T = din("w1T", [D, FH])
    w2T = din("w2T", [D, FH])
    w3T = din("w3T", [FH, D])
    convw = din("convw", [128, 16])
    convb = din("convb", [128, NCT])
    bdt = din("bdt", [128, NCT])
    aflat = din("aflat", [128, NCT])
    dvecd = din("dvec", [128, NCT])
    flag_d = din("flag", [128, 1])

    out_d = nc.dram_tensor("out", [T, D], F32, kind="ExternalOutput").ap()

    _dbg = "ExternalOutput" if os.environ.get("KDBG") else "Internal"
    x1tm_sp = nc.dram_tensor("x1tm_sp", [T, D], F32, kind=_dbg).ap()
    x1cm_sp = nc.dram_tensor("x1cm_sp", [NKD, 128, T], BF16, kind=_dbg).ap()
    r2_sp = nc.dram_tensor("r2_sp", [T, D], F32, kind=_dbg).ap()
    q1s = nc.dram_tensor("q1s", [D, FH], BF16, kind=_dbg).ap()
    q2s = nc.dram_tensor("q2s", [D, FH], BF16, kind=_dbg).ap()
    q3s = nc.dram_tensor("q3s", [FH, D], BF16, kind=_dbg).ap()

    S = Sched(nc)
    ctx = ExitStack()
    with ctx:
        sb = lambda name, shape, dt=F32: ctx.enter_context(nc.sbuf_tensor(name, shape, dt))

        consts = sb("consts", [128, 4])
        ones_col = sb("ones_col", [128, 1])
        ones_row = sb("ones_row", [1, 128])
        gtmp = sb("gtmp", [1, 1])
        gscale = sb("gscale", [128, 9])
        smallw = sb("smallw", [128, 33])
        cw = smallw[:, 0:16]
        cb = smallw[:, 16:20]
        bd = smallw[:, 20:24]
        afl = smallw[:, 24:28]
        dv = smallw[:, 28:32]
        fl = smallw[:, 32:33]
        a_neg = sb("a_neg", [128, NCT])
        state_aux = sb("state_aux", [128, NCT])
        state_own = sb("state_own", [128, NCT])
        sums = sb("sums", [128, 16])
        sumsq = sb("sumsq", [128, 16])
        negm = sb("negm", [128, 16])
        rstd = sb("rstd", [128, 16])
        rstdh = sb("rstdh", [128, 16])
        lnt = sb("lnt", [128, 4])
        lnv = sb("lnv", [128, 4])
        parts = sb("parts", [128, 2])
        gcols = sb("gcols", [128, 344])

        ident = sb("ident", [128, 128], BF16)
        ps8 = ctx.enter_context(nc.psum_tensor("ps8", [128, 8, 512], F32))
        ptp = ps8[:, 0:2, :].bitcast(BF16)

        mctx = ExitStack()
        msb = lambda name, shape, dt=F32: mctx.enter_context(nc.sbuf_tensor(name, shape, dt))
        winx = msb("winx", [128, NKD, DI], BF16)
        winz = msb("winz", [128, NKD, DI], BF16)
        wpar = msb("wpar", [128, NCT, 1152], BF16)
        wdt = msb("wdt", [128, DI], BF16)
        wout = msb("wout", [128, NCT, D], BF16)
        hsb = msb("hsb", [128, 3, 8, TC], BF16)
        x_sb = msb("x_sb", [128, NCT, TC + 3])
        xtail = msb("xtail", [128, NCT, 3])
        xact = msb("xact", [128, NCT, TC])
        xactb = msb("xactb", [128, NCT, TC], BF16)
        sz = msb("sz", [128, NCT, TC])
        bmat = msb("bmat", [128, NCT, TC])
        cmat = msb("cmat", [128, NCT, TC])
        dtb = msb("dtb", [128, TC], BF16)
        delta = msb("delta", [128, NCT, TC])
        a_sb = msb("a_sb", [128, NCT, TC])
        h_sb = msb("h_sb", [128, NCT, TC])
        y3 = msb("y3", [128, NCT, TC], BF16)
        r1 = msb("r1", [128, 2, D])
        x1bf = msb("x1bf", [128, 4, D], BF16)
        tpst = msb("tpst", [128, D], BF16)
        junk2k = msb("junk2k", [128, D], BF16)
        gstage = [msb(f"gstage_{i}", [128, 704]) for i in range(3)]
        mctx.close()

        x1cm = sb("x1cm", [128, NKD, TH], BF16)
        hidden = sb("hidden", [128, NFC, TH], BF16)
        qb = {w: [sb(f"qb_{w}_{i}", [128, 16, FWIN], BF16) for i in range(2)]
              for w in ("w1", "w2")}
        sgnb = sb("sgnb", [128, 1024], BF16)
        sgnbA = sgnb.rearrange("p (a b) -> p a b", a=8)
        wstage = [sb(f"wstage_{i}", [128, 8, FWIN]) for i in range(4)]
        tmpq = sb("tmpq", [128, 2, 8, FWIN], BF16)
        sigb = [sb(f"sigb_{i}", [128, TH], BF16) for i in range(2)]
        q3b = [sb(f"q3b_{i}", [128, 2, 512], BF16) for i in range(3)]
        w3stage = [sb(f"w3stage_{i}", [128, 2, 512]) for i in range(2)]
        x1tmld = [sb(f"x1tmld_{i}", [128, 512]) for i in range(2)]

        r2t = [sb(f"r2t_{i}", [128, 512]) for i in range(2)]
        junk512 = sb("junk512", [128, 512])
        r2full = [sb("r2full_0", [128, D])]
        outst = [sb("outst_0", [128, D])]
        g3slots = [r2full[0][:, 0:1024], r2full[0][:, 1024:2048],
                   outst[0][:, 0:1024], outst[0][:, 1024:2048]]

        for name in ("dHS0", "dHS1", "dHS2", "dHSTM0", "dHSTM1", "dWM",
                     "dWX", "dWZ", "dWP", "dWO", "dWD",
                     "dG0", "dG1", "dG3", "dX1CM",
                     "dW12_100", "dW12_101", "dW12_110", "dW12_111",
                     "dW12_200", "dW12_201", "dW12_210", "dW12_211",
                     "dW3s0", "dW3s1", "dW3s2", "dW3q0", "dW3q1", "dW3q2",
                     "dXT0", "dXT1",
                     "dR2RD", "dOUT", "wXT0", "wXT1", "wXC", "pTP", "vTPC",
                     "wQ12_0", "wQ12_1", "wQ3_0", "wQ3_1", "wQ3_2",
                     "wR2_0", "wR2_1",
                     "pIPx", "pIPz", "pPAR", "pDT", "pWOUT", "pGM",
                     "pA", "pB", "pBf",
                     "gG", "gGa", "gG3", "dG30", "dG31", "dG32", "dG33", "dG2", "vGS",
                     "vEVX", "vXB", "vPAR", "vB", "vSCAN", "vY3", "vR1", "vX1",
                     "vQADD", "vHID", "vR2S", "vR2EV", "vNRM",
                     "aZ", "aXA", "aDELTA", "aAA", "aSQ", "aRSTD", "aSGN", "aSIG",
                     "init"):
            S.sem(ctx, name)

        def init_fn(e):
            e.memset(ident[:], 0.0)
            e.affine_select(out=ident[:], in_=ident[:], compare_op=ALU.not_equal,
                            fill=1.0, base=0, pattern=[[-1, 128]], channel_multiplier=1)
            e.memset(consts[:, 0:1], -0.5)
            e.memset(consts[:, 1:2], 0.5)
            e.memset(consts[:, 2:3], 1.0)
            e.memset(consts[:, 3:4], EPS_LN)
            e.memset(ones_col[:], 1.0)
            return e.memset(ones_row[:], 1.0)
        S.op("pool", init_fn, inc="init")

        S.dma("sp", smallw[:, 0:16], convw[:], inc="dWM")
        S.dma("sp", cb, convb[:], inc="dWM")
        S.dma("sp", bd, bdt[:], inc="dWM")
        S.dma("sp", afl, aflat[:], inc="dWM")
        S.dma("sp", dv, dvecd[:], inc="dWM")
        S.dma("sp", fl, flag_d[:], inc="dWM")
        n_wm = S.c("dWM")

        winx_r = winxT.rearrange("(j p) f -> p j f", p=128)
        for j0 in range(0, NKD, 4):
            S.dma("pool", winx[:, j0:j0 + 4, :], winx_r[:, j0:j0 + 4, :],
                  waits=[("init", 1)], inc="dWX")
        n_winx = S.c("dWX")

        def emit_hs_half(h):
            g = h // 2
            j = h % 2
            sl = h % 3
            src = hs_cm if g >= 4 else aux_cm
            ci = g % 4
            src_r = src.rearrange("(k p) t -> p k t", p=128)
            w = []
            hprev = h - 3
            if hprev >= 0:
                gp = hprev // 2
                if gp <= 3:
                    w.append(("pIPx", gp + 1))
                else:
                    w.append(("pIPz", gp - 3))
            for q in range(2):
                k0 = 8 * j + 4 * q
                S.dma("pool", hsb[:, sl, 4 * q:4 * q + 4, :],
                      src_r[:, k0:k0 + 4, ci * TC:(ci + 1) * TC],
                      waits=(w if q == 0 else []), inc="dHS%d" % sl)

        gma = {"dma": 0, "red": 0, "dma3": 0, "red3": 0}
        w3r = w3T.rearrange("(j p) d -> p j d", p=128)
        gtiles = []
        for wi, src in ((0, w1T), (1, w2T)):
            for blk in range(16):
                for q in range(8):
                    gtiles.append((src[blk * 128:(blk + 1) * 128,
                                       q * 704:(q + 1) * 704], 704))
        NGT = len(gtiles)

        def g_eng(gi):
            return "dve" if (gi // 8) % 2 == 0 else "act"

        def g_cnt(gi):
            full, rem = (gi + 1) // 16, (gi + 1) % 16
            if g_eng(gi) == "dve":
                return full * 8 + min(rem, 8)
            return full * 8 + max(0, rem - 8)

        def g_wait(gi):
            return (("gG" if g_eng(gi) == "dve" else "gGa"), g_cnt(gi))

        g3tiles = []
        for j in range(44):
            for k in range(2):
                g3tiles.append((w3r[:, j, k * 1024:(k + 1) * 1024], 1024))

        def gamma_dma(n):
            for _ in range(n):
                gi = gma["dma"]
                if gi >= NGT:
                    return
                tile, width = gtiles[gi]
                wv = [g_wait(gi - 3)] if gi >= 3 else []
                S.dma("sp", gstage[gi % 3][:, 0:width], tile,
                      waits=wv, inc="dG%d" % (gi % 3))
                gma["dma"] += 1

        def gamma_reduce(n, eng="dve"):
            for _ in range(n):
                gi = gma["red"]
                if gi >= NGT:
                    return
                width = gtiles[gi][1]
                if g_eng(gi) == "dve":
                    S.op("dve",
                         (lambda g, cc, wd: (lambda e: e.tensor_reduce(
                             gcols[:, cc:cc + 1], gstage[g % 3][:, 0:wd], AX.X,
                             ALU.add, apply_absolute_value=True)))(gi, gi, width),
                         waits=[("dG%d" % (gi % 3), 16 * (gi // 3 + 1))], inc="gG")
                else:
                    S.op("act",
                         (lambda g, cc, wd: (lambda e: e.activation(
                             junk2k[:, 0:wd], gstage[g % 3][:, 0:wd], AF.Abs,
                             accum_out=gcols[:, cc:cc + 1])))(gi, gi, width),
                         waits=[("dG%d" % (gi % 3), 16 * (gi // 3 + 1))], inc="gGa")
                gma["red"] += 1

        def gamma3_batch(n, slots):
            for _ in range(n):
                gi = gma["dma3"]
                if gi >= 88:
                    break
                tile, width = g3tiles[gi]
                S.dma("sp", slots[gi % 4][:, 0:width],
                      tile, waits=[("gG3", max(0, gi - 3))],
                      inc="dG3%d" % (gi % 4))
                gma["dma3"] += 1
            batch = []
            for _ in range(n):
                gi = gma["red3"]
                if gi >= 88:
                    break
                batch.append(gi)
                gma["red3"] += 1
            if batch:
                bw = [("dG3%d" % (gi % 4), 16 * (gi // 4 + 1)) for gi in batch]

                def red_fn(tiles):
                    def f(e):
                        ii = None
                        for gi in tiles:
                            wd = g3tiles[gi][1]
                            ii = e.tensor_reduce(
                                gcols[:, 256 + gi:257 + gi],
                                slots[gi % 4][:, 0:wd], AX.X,
                                ALU.add, apply_absolute_value=True)
                        return ii
                    return f
                S.op("dve", red_fn(list(batch)), waits=bw,
                     inc="gG3", amt=len(batch), relax=4)

        def gamma_final(wi, extra=()):
            g0, g1 = ((0, 128), (128, 256), (256, 344))[wi]
            if wi == 2:
                gw = [("gG3", 88)]
            else:
                gw = [("gG", g_cnt(g1 - 1) if g_eng(g1 - 1) == "dve" else g1 // 2),
                      ("gGa", g1 // 2)]
                gw = [("gG", g1 // 2), ("gGa", g1 // 2)]
            S.op("dve", (lambda a, b: (lambda e: e.tensor_reduce(
                lnt[:, 0:1], gcols[:, a:b], AX.X, ALU.add)))(g0, g1),
                 waits=gw + [("pGM", S.c("pGM"))] + list(extra), inc="vGS")
            vgs = S.c("vGS")
            S.op("pe", lambda e: e.matmul(ps8[:1, 7, 0:1], lnt[:, 0:1], ones_col[:],
                                          start=True, stop=True),
                 waits=[("vGS", vgs), ("aDELTA", S.c("aDELTA")),
                        ("vR1", S.c("vR1"))], inc="pGM")
            pg = S.c("pGM")
            S.op("dve", lambda e: e.tensor_copy(gtmp[:], ps8[:1, 7, 0:1]),
                 waits=[("pGM", pg)], inc="vGS")
            vgs2 = S.c("vGS")
            S.op("pe", lambda e: e.matmul(ps8[:, 7, 1:2], ones_row[:], gtmp[:],
                                          start=True, stop=True),
                 waits=[("vGS", vgs2)], inc="pGM")
            pg2 = S.c("pGM")
            S.op("dve", (lambda w: (lambda e: e.tensor_scalar(
                gscale[:, w:w + 1], ps8[:, 7, 1:2], 1.0 / NW_ELEM, EPS_Q,
                ALU.mult, ALU.add)))(wi),
                 waits=[("pGM", pg2)], inc="vGS")
            S.op("dve", (lambda w: (lambda e: e.tensor_scalar(
                gscale[:, 3 + w:4 + w], gscale[:, w:w + 1], 0.5, None,
                ALU.mult)))(wi), inc="vGS")
            S.op("dve", (lambda w: (lambda e: e.tensor_scalar(
                gscale[:, 6 + w:7 + w], gscale[:, w:w + 1], -0.5, None,
                ALU.mult)))(wi), inc="vGS")
            S.op("dve", (lambda w: (lambda e: e.reciprocal(
                gscale[:, w:w + 1], gscale[:, w:w + 1])))(wi), inc="vGS")

        S.op("act", lambda e: e.activation(a_neg[:], afl, AF.Exp),
             waits=[("dWM", n_wm)], inc="aAA")
        S.op("dve", lambda e: e.tensor_scalar_mul(a_neg[:], a_neg[:], -1.0),
             waits=[("aAA", 1)], inc="vB")

        def emit_rest_weights():
            for (dst, src, nj, sem) in ((winz, winzT, NKD, "dWZ"),
                                        (wpar, wparamT, NCT, "dWP"),
                                        (wout, woutT, NCT, "dWO")):
                src_r = src.rearrange("(j p) f -> p j f", p=128)
                step = max(1, nj // 2)
                for j0 in range(0, nj, step):
                    j1 = min(nj, j0 + step)
                    S.dma("pool", dst[:, j0:j1, :], src_r[:, j0:j1, :],
                          inc=sem)
            S.dma("pool", wdt[:], wdtT[:], inc="dWD")

        mst = {"tsub": 0}
        vx1_of = {}

        def own(g):
            return g >= 4

        def emit_ipx(g):
            ci = g % 4
            w = [("dHS%d" % ((2 * g) % 3), 32 * ((2 * g) // 3 + 1)),
                 ("dHS%d" % ((2 * g + 1) % 3), 32 * ((2 * g + 1) // 3 + 1)),
                 ("vEVX", max(0, 2 * g - 1)),
                 ("vTPC", S.c("vTPC"))]
            if g == 0:
                w += [("dWX", 64)]

            def f(e):
                ii = None
                for k in range(NKD):
                    sl = (2 * g + k // 8) % 3
                    for c in range(NCT):
                        ii = e.matmul(ps8[:, c, :], winx[:, k, c * 128:(c + 1) * 128],
                                      hsb[:, sl, k % 8, :], start=(k == 0),
                                      stop=(k == NKD - 1))
                return ii
            S.op("pe", f, waits=w, inc="pIPx")

        def emit_ipz(g):
            w = [("vR1", S.c("vR1")), ("vPAR", S.c("vPAR")),
                 ("aDELTA", S.c("aDELTA")), ("aZ", S.c("aZ")),
                 ("vGS", S.c("vGS")), ("dWZ", 32)]

            def f(e):
                ii = None
                for k in range(NKD):
                    sl = (2 * g + k // 8) % 3
                    for c in range(NCT):
                        ii = e.matmul(ps8[:, 4 + c, :],
                                      winz[:, k, c * 128:(c + 1) * 128],
                                      hsb[:, sl, k % 8, :], start=(k == 0),
                                      stop=(k == NKD - 1))
                return ii
            S.op("pe", f, waits=w, inc="pIPz")

        def emit_evacx(g):
            ci = g % 4
            cvw = [("vEVX", 2 * g)] if g > 0 else []
            if g == 0:
                S.op("dve", lambda e: e.memset(x_sb[:, :, 0:3], 0.0))
            elif g == 4:
                S.op("dve", lambda e: e.tensor_scalar(
                    x_sb[:, :, 0:3], xtail[:], fl, None, ALU.mult), waits=cvw)
            else:
                S.op("dve", lambda e: e.tensor_copy(x_sb[:, :, 0:3],
                                                    x_sb[:, :, TC:TC + 3]),
                     waits=cvw)
            S.op("dve", lambda e: e.tensor_copy(x_sb[:, :, 3:TC + 3],
                                                ps8[:, 0:NCT, :]),
                 waits=[("pIPx", g + 1)], inc="vEVX")
            if g == 3:
                S.op("dve", lambda e: e.tensor_copy(xtail[:], x_sb[:, :, TC:TC + 3]))

        def emit_conv_silu_cast(g):
            def conv_fn(e):
                ii = None
                for c in range(NCT):
                    ii = e.tensor_scalar(xact[:, c, :], x_sb[:, c, 0:TC],
                                         cw[:, 4 * c:4 * c + 1], None, ALU.mult)
                for k in range(1, 4):
                    for c in range(NCT):
                        ii = e.scalar_tensor_tensor(
                            xact[:, c, :], x_sb[:, c, k:k + TC],
                            cw[:, 4 * c + k:4 * c + k + 1], xact[:, c, :],
                            ALU.mult, ALU.add)
                return ii
            cw_w = [("vB", S.c("vB")), ("vY3", S.c("vY3"))]
            if g == 0:
                cw_w.append(("dWM", n_wm))
            S.op("dve", conv_fn, waits=cw_w, inc="vEVX", relax=4)
            vcv = S.c("vEVX")

            def silu_fn(e):
                ii = None
                for c in range(NCT):
                    ii = e.activation(xact[:, c, :], xact[:, c, :], AF.Silu,
                                      bias=cb[:, c:c + 1], scale=1.0)
                return ii
            S.op("act", silu_fn, waits=[("vEVX", vcv)], inc="aXA", relax=4)
            S.op("dve", lambda e: e.tensor_copy(xactb[:], xact[:]),
                 waits=[("aXA", S.c("aXA")), ("pPAR", S.c("pPAR"))], inc="vXB")

        def emit_siluz(g):
            S.op("act", lambda e: e.activation(sz[:], ps8[:, 4:8, :], AF.Silu),
                 waits=[("pIPz", g - 3), ("vY3", S.c("vY3"))], inc="aZ")

        def emit_params(g):
            npar = NPAR if own(g) else NPAR_AUX
            vxb = S.c("vXB")
            vp0 = S.c("vPAR")

            def par_mm(mm):
                bank = 4 + mm % 4
                def f(e):
                    ii = None
                    for c in range(NCT):
                        ii = e.matmul(ps8[:, bank, :],
                                      wpar[:, c, mm * 128:(mm + 1) * 128],
                                      xactb[:, c, :], start=(c == 0),
                                      stop=(c == NCT - 1))
                    return ii
                return f

            for m in range(npar):
                w = []
                if m == 0:
                    w = [("vXB", vxb), ("pDT", S.c("pDT")),
                         ("vPAR", S.c("vPAR")), ("vR1", S.c("vR1")),
                         ("dWP", 32)]
                    if own(g):
                        w.append(("aZ", S.c("aZ")))
                elif m == 4:
                    w = [("vPAR", vp0 + 1)]
                elif m == 5:
                    w = [("vPAR", vp0 + 2)]
                elif m == 8:
                    w = [("vPAR", vp0 + 3)]
                S.op("pe", par_mm(m), waits=w, inc="pPAR")
                pp = S.c("pPAR")
                if m == 0:
                    S.op("dve", lambda e: e.tensor_copy(dtb[:], ps8[:, 4, :]),
                         waits=[("pPAR", pp)], inc="vPAR")
                elif m == 3:
                    S.op("dve", lambda e: e.tensor_copy(bmat[:, 0:3, :],
                                                        ps8[:, 5:8, :]),
                         waits=[("pPAR", pp)], inc="vPAR")
                elif m == 4:
                    S.op("dve", lambda e: e.tensor_copy(bmat[:, 3, :],
                                                        ps8[:, 4, :]),
                         waits=[("pPAR", pp)], inc="vPAR")
                elif m == 7:
                    S.op("dve", lambda e: e.tensor_copy(cmat[:, 0:3, :],
                                                        ps8[:, 5:8, :]),
                         waits=[("pPAR", pp)], inc="vPAR")
                elif m == 8:
                    S.op("dve", lambda e: e.tensor_copy(cmat[:, 3, :],
                                                        ps8[:, 4, :]),
                         waits=[("pPAR", pp)], inc="vPAR")

        def emit_dt(g):
            vpar = S.c("vPAR")
            for c in range(NCT):
                w = []
                if c == 0:
                    w = [("vPAR", vpar), ("dWD", 16)]
                S.op("pe", (lambda bk, cc: (lambda e: e.matmul(
                    ps8[:, bk, :], wdt[:, cc * 128:(cc + 1) * 128], dtb[:],
                    start=True, stop=True)))(4 + c, c),
                    waits=w, inc="pDT")
            pd = S.c("pDT")

            def exp_fn(e):
                ii = None
                for c in range(NCT):
                    ii = e.activation(delta[:, c, :], ps8[:, 4 + c, :], AF.Exp,
                                      bias=bd[:, c:c + 1], scale=1.0)
                return ii
            S.op("act", exp_fn, waits=[("pDT", pd), ("vB", S.c("vB"))],
                 relax=4)

            def ln_fn(e):
                ii = None
                for c in range(NCT):
                    ii = e.activation(delta[:, c, :], delta[:, c, :], AF.Ln,
                                      bias=consts[:, 2:3], scale=1.0)
                return ii
            S.op("act", ln_fn, inc="aDELTA", relax=4)

        def emit_scan(g):
            adel = S.c("aDELTA")
            state = state_own if own(g) else state_aux

            def ab_wide(e):
                e.tensor_tensor(bmat[:], delta[:], bmat[:], ALU.mult)
                return e.tensor_tensor(bmat[:], bmat[:], xact[:], ALU.mult)
            S.op("dve", ab_wide, waits=[("aDELTA", adel)], inc="vB")

            def ab_dn(e):
                ii = None
                for c in range(NCT):
                    ii = e.tensor_scalar(delta[:, c, :], delta[:, c, :],
                                         a_neg[:, c:c + 1], None, ALU.mult)
                return ii
            S.op("dve", ab_dn, inc="vB", relax=4)

            def a_fn(e):
                ii = None
                for c in range(NCT):
                    ii = e.activation(a_sb[:, c, :], delta[:, c, :], AF.Exp)
                return ii
            S.op("act", a_fn, waits=[("vB", S.c("vB")), ("vSCAN", S.c("vSCAN"))],
                 inc="aAA", relax=4)

            if g == 4:
                S.op("dve", lambda e: e.tensor_scalar(
                    state_own[:], state_aux[:], fl, None, ALU.mult))

            zero_init = (g == 0)

            def scan_fn(e):
                for c in range(NCT):
                    init = 0.0 if zero_init else state[:, c:c + 1]
                    e.tensor_tensor_scan(h_sb[:, c, :], a_sb[:, c, :],
                                         bmat[:, c, :], init, ALU.mult,
                                         ALU.add)
                ii = None
                for c in range(NCT):
                    ii = e.tensor_copy(state[:, c:c + 1], h_sb[:, c, TC - 1:TC])
                return ii
            S.op("dve", scan_fn, waits=[("aAA", S.c("aAA"))], inc="vSCAN",
                 relax=4)

        def emit_y3(g):
            S.op("dve", lambda e: e.tensor_tensor(cmat[:], cmat[:], h_sb[:],
                                                  ALU.mult),
                 waits=[("vSCAN", S.c("vSCAN")), ("aZ", S.c("aZ")),
                        ("pWOUT", S.c("pWOUT"))])

            def y_a(e):
                ii = None
                for c in range(NCT):
                    ii = e.scalar_tensor_tensor(cmat[:, c, :], xact[:, c, :],
                                                dv[:, c:c + 1], cmat[:, c, :],
                                                ALU.mult, ALU.add)
                return ii
            S.op("dve", y_a, relax=4)
            S.op("dve", lambda e: e.tensor_tensor(y3[:], cmat[:], sz[:],
                                                  ALU.mult), inc="vY3")

        def emit_wout_tail(g):
            ci = g - 4
            vy3 = S.c("vY3")
            for pair in range(2):
                for s in (2 * pair, 2 * pair + 1):
                    tt = 4 * ci + s
                    trow = ci * TC + s * 128
                    S.dma("act", r1[:, tt % 2, :], hs_tm[trow:trow + 128, :],
                          waits=[("wXT%d" % (tt % 2),
                                  16 * max(0, tt // 2))], inc="dHSTM%d" % (tt % 2))
                    for dh in range(2):
                        w = [("vY3", vy3)]
                        w.append(("vR1", max(0, S.c("vR1") - 1)))
                        if s == 0 and dh == 0:
                            w += [("vPAR", S.c("vPAR")),
                                  ("aDELTA", S.c("aDELTA")),
                                  ("vGS", S.c("vGS")), ("dWO", 32)]

                        def wo_fn(ss, dd):
                            def f(e):
                                ii = None
                                for n in range(2):
                                    col = 2 * dd + n
                                    for c in range(NCT):
                                        ii = e.matmul(
                                            ps8[:, 4 + 2 * dd + n, :],
                                            y3[:, c, ss * 128:(ss + 1) * 128],
                                            wout[:, c, col * 512:(col + 1) * 512],
                                            start=(c == 0), stop=(c == NCT - 1))
                                return ii
                            return f
                        S.op("pe", wo_fn(s, dh), waits=w, inc="pWOUT")
                        pw = S.c("pWOUT")
                        S.op("dve", (lambda ttt, dd: (lambda e: e.scalar_tensor_tensor(
                            r1[:, ttt % 2, dd * 1024:(dd + 1) * 1024],
                            ps8[:, 4 + 2 * dd:6 + 2 * dd, :].rearrange("p a b -> p (a b)"),
                            1.0, r1[:, ttt % 2, dd * 1024:(dd + 1) * 1024],
                            ALU.mult, ALU.add,
                            accum_out=parts[:, dd:dd + 1])))(tt, dh),
                            waits=[("pWOUT", pw),
                                   ("dHSTM%d" % (tt % 2), 16 * (tt // 2 + 1))],
                            inc="vR1")
                    S.op("dve", (lambda ttt: (lambda e: e.tensor_tensor(
                        sums[:, ttt:ttt + 1], parts[:, 0:1], parts[:, 1:2],
                        ALU.add)))(tt))
                    S.op("act", (lambda ttt: (lambda e: e.activation(
                        junk2k[:], r1[:, ttt % 2, :], AF.Square,
                        accum_out=sumsq[:, ttt:ttt + 1])))(tt),
                        waits=[("vR1", S.c("vR1"))], inc="aSQ")

                t0 = 4 * ci + 2 * pair
                asq = S.c("aSQ")
                S.op("dve", (lambda tz: (lambda e: e.tensor_scalar(
                    negm[:, tz:tz + 2], sums[:, tz:tz + 2], -1.0 / D, None,
                    ALU.mult)))(t0),
                    waits=[("aSQ", asq)])
                S.op("dve", (lambda tz: (lambda e: e.tensor_tensor(
                    lnv[:, 0:2], negm[:, tz:tz + 2], negm[:, tz:tz + 2],
                    ALU.mult)))(t0))
                S.op("dve", (lambda tz: (lambda e: e.scalar_tensor_tensor(
                    lnv[:, 0:2], sumsq[:, tz:tz + 2], 1.0 / D, lnv[:, 0:2],
                    ALU.mult, ALU.subtract)))(t0), inc="vNRM")
                vn = S.c("vNRM")
                S.op("act", lambda e: e.activation(lnv[:, 0:2], lnv[:, 0:2],
                                                   AF.Ln, bias=consts[:, 3:4],
                                                   scale=1.0),
                     waits=[("vNRM", vn)])
                S.op("act", (lambda tz: (lambda e: e.activation(
                    rstd[:, tz:tz + 2], lnv[:, 0:2], AF.Exp, scale=-0.5)))(t0),
                    inc="aRSTD")
                S.op("dve", (lambda tz: (lambda e: e.tensor_scalar(
                    rstdh[:, tz:tz + 2], rstd[:, tz:tz + 2], 0.5, None,
                    ALU.mult)))(t0),
                    waits=[("aRSTD", S.c("aRSTD"))])

                for s in (2 * pair, 2 * pair + 1):
                    tt = 4 * ci + s
                    trow = ci * TC + s * 128
                    S.op("dve", (lambda ttt: (lambda e: e.tensor_scalar(
                        x1bf[:, ttt % 4, :], r1[:, ttt % 2, :],
                        negm[:, ttt:ttt + 1], rstdh[:, ttt:ttt + 1],
                        ALU.add, ALU.mult)))(tt),
                        waits=[("pTP", max(0, tt - 3))], inc="vX1")
                    vx1_of[tt] = S.c("vX1")
                    S.op("dve", (lambda ttt: (lambda e: e.tensor_scalar(
                        r1[:, ttt % 2, :], r1[:, ttt % 2, :],
                        negm[:, ttt:ttt + 1], rstd[:, ttt:ttt + 1],
                        ALU.add, ALU.mult)))(tt), inc="vX1")
                    vx1 = S.c("vX1")
                    S.dma("act", x1tm_sp[trow:trow + 128, :], r1[:, tt % 2, :],
                          waits=[("vX1", vx1)], inc="wXT%d" % (tt % 2))

        def emit_tp(j, vevx_need):
            ci = j - 4
            for s in range(4):
                tt = 4 * ci + s
                trow = ci * TC + s * 128
                w = [("vX1", vx1_of[tt]), ("vTPC", tt)]
                if s == 0:
                    w.append(("vEVX", vevx_need))

                def tp_fn(ttt):
                    def f(e):
                        ii = None
                        for jj in range(NKD):
                            ii = e.transpose(
                                ptp[:, jj // 8, (jj % 8) * 128:(jj % 8 + 1) * 128],
                                x1bf[:, ttt % 4, jj * 128:(jj + 1) * 128], ident[:])
                        return ii
                    return f
                S.op("pe", tp_fn(tt), waits=w, inc="pTP")
                S.op("dve", (lambda ttt: (lambda e: e.tensor_copy(
                    tpst[:], ptp.rearrange("p a b -> p (a b)"))))(tt),
                    waits=[("pTP", S.c("pTP")), ("wXC", 32 * tt)],
                    inc="vTPC")
                vtpc = S.c("vTPC")
                tps3 = tpst.rearrange("p (jj t) -> p jj t", t=128)
                for jh in range(2):
                    S.dma("act",
                          x1cm_sp.rearrange("j p t -> p j t")[
                              :, jh * 8:(jh + 1) * 8, trow:trow + 128],
                          tps3[:, jh * 8:(jh + 1) * 8, :],
                          waits=([("vTPC", vtpc)] if jh == 0 else []),
                          inc="wXC")

        emit_hs_half(0)
        emit_hs_half(1)
        emit_hs_half(2)
        emit_rest_weights()

        gamma_dma(4)

        def chunk_vector_chain(g):
            emit_conv_silu_cast(g)
            if own(g):
                emit_siluz(g)
            emit_params(g)
            emit_dt(g)
            emit_scan(g)
            if own(g):
                emit_y3(g)

        if STAGE >= 1:
            emit_ipx(0)
            emit_hs_half(3)
            emit_ipx(1)
            for g in range(4):
                emit_evacx(g)
                chunk_vector_chain(g)
                gamma_reduce(40)
                gamma_dma(40)
                if g + 2 <= 3:
                    emit_hs_half(2 * (g + 2))
                    emit_hs_half(2 * (g + 2) + 1)
                    emit_ipx(g + 2)
                elif g + 2 == 4:
                    emit_hs_half(8)
                    emit_hs_half(9)
                    emit_ipx(4)

        KSUB = os.environ.get("KSUB", "e")
        DO_WOUT = KSUB >= "b"
        DO_TP = KSUB >= "d"
        DO_FIN = KSUB >= "e"
        if STAGE >= 2:
            for g in range(4, 8):
                emit_evacx(g)
                if g >= 6 and DO_TP:
                    emit_tp(g - 2, 2 * g + 1)
                if g > 4 and DO_WOUT:
                    emit_wout_tail(g - 1)
                emit_ipz(g)
                chunk_vector_chain(g)
                gamma_reduce(24)
                gamma_dma(24)
                if g + 1 <= 7:
                    emit_hs_half(2 * (g + 1))
                    emit_hs_half(2 * (g + 1) + 1)
                    emit_ipx(g + 1)
            if DO_TP:
                emit_tp(6, 15)
            if DO_WOUT:
                emit_wout_tail(7)
            if DO_TP:
                emit_tp(7, 15)
            gamma_dma(256)
            gamma_reduce(256)
            if DO_FIN:
                gamma_final(0)
                gamma_final(1)
        else:
            gamma_dma(256)
            gamma_reduce(256)
            gamma_final(0)
            gamma_final(1)

        BAR = [("pGM", S.c("pGM")), ("vGS", S.c("vGS")),
               ("aRSTD", S.c("aRSTD")), ("pTP", S.c("pTP")), ("vTPC", S.c("vTPC")),
               ("wXT0", S.c("wXT0")), ("wXT1", S.c("wXT1")), ("wXC", S.c("wXC")),
               ("vR1", S.c("vR1")), ("vX1", S.c("vX1")),
               ("vY3", S.c("vY3")), ("vSCAN", S.c("vSCAN")),
               ("pWOUT", S.c("pWOUT")), ("pPAR", S.c("pPAR")),
               ("pDT", S.c("pDT")), ("pIPx", S.c("pIPx")),
               ("pIPz", S.c("pIPz")), ("aZ", S.c("aZ")),
               ("aXA", S.c("aXA")), ("vXB", S.c("vXB")),
               ("aSQ", S.c("aSQ")), ("gG", S.c("gG")), ("gGa", S.c("gGa")),
               ("dWM", n_wm), ("dWX", 64), ("dWZ", 32), ("dWP", 32),
               ("dWO", 32), ("dWD", 16), ("init", 1)]

        w12src = {"w1": w1T, "w2": w2T}
        qsdst = {"w1": q1s, "w2": q2s}
        gsi = {"w1": 0, "w2": 1}
        wQ_A = {}
        fst = {"x1tmld": 0, "r2ev": 0}

        def ffn_half(h, sub=2):
            for j in range(NKD):
                w = []
                if j == 0:
                    w = list(BAR) if h == 0 else [("pA", 88)]
                S.dma("sp", x1cm[:, j, :], x1cm_sp[j, :, h * TH:(h + 1) * TH],
                      waits=w, inc="dX1CM")
            dx1 = S.c("dX1CM")
            if sub < 1:
                return

            gidx = [0]
            vq_of = {}
            dw_of = {}
            sig_of = {}
            hid_of = {}
            pa_of = {}

            def a_quant(win):
                for wname in ("w1", "w2"):
                    for g in range(2):
                        qdst = qb[wname][win % 2][:, g * 8:(g + 1) * 8, :]
                        if h == 0:
                            src = w12src[wname].rearrange("(j p) f -> p j f", p=128)
                            stg = wstage[gidx[0] % 4]
                            gidx[0] += 1
                            wst = [("aSGN", max(0, S.c("aSGN") - 3))]
                            if win == 0 and wname == "w1" and g == 0:
                                wst = wst + BAR
                            slot = "dW12_%s%d%d" % (wname[1], g, win % 2)
                            S.dma("sp", stg[:],
                                  src[:, g * 8:(g + 1) * 8,
                                      win * FWIN:(win + 1) * FWIN],
                                  waits=wst, inc=slot)
                            qwar = [(slot, S.c(slot))]
                            if win >= 2:
                                qwar += [("pA", 2 * (win - 1)),
                                         ("wQ12_%d" % (win % 2), 64 * (win // 2))]
                            wi = gsi[wname]
                            if wname == "w1":
                                sc = gscale[:, wi:wi + 1]
                                S.op("act", (lambda st, qd, s2: (lambda e: e.activation(
                                    qd, st[:], AF.Sign, bias=consts[:, 0:1],
                                    scale=s2)))(stg, qdst, sc),
                                    waits=qwar)
                                S.op("act", (lambda st, s2: (lambda e: e.activation(
                                    sgnbA[:], st[:], AF.Sign, bias=consts[:, 1:2],
                                    scale=s2)))(stg, sc),
                                    waits=[("vQADD", S.c("vQADD"))], inc="aSGN")
                                S.op("dve", (lambda qd: (lambda e: e.tensor_tensor(
                                    qd, qd, sgnbA[:], ALU.add)))(qdst),
                                    waits=[("aSGN", S.c("aSGN"))], inc="vQADD")
                            else:
                                def cmp_fn(st, w2):
                                    def f(e):
                                        e.tensor_scalar(
                                            tmpq[:, 0, :, :], st,
                                            gscale[:, 3 + w2:4 + w2],
                                            2.0, ALU.is_gt, ALU.mult)
                                        return e.tensor_scalar(
                                            tmpq[:, 1, :, :], st,
                                            gscale[:, 6 + w2:7 + w2],
                                            2.0, ALU.is_lt, ALU.mult)
                                    return f
                                S.op("dve", cmp_fn(stg[:], wi), waits=qwar,
                                     inc="aSGN", relax=2)
                                S.op("dve", (lambda qd: (lambda e: e.tensor_tensor(
                                    qd, tmpq[:, 0, :, :], tmpq[:, 1, :, :],
                                    ALU.subtract)))(qdst),
                                    inc="vQADD")
                            vq = S.c("vQADD")
                            S.dma("pool",
                                  qsdst[wname].rearrange("(j p) f -> p j f", p=128)[
                                      :, g * 8:(g + 1) * 8,
                                      win * FWIN:(win + 1) * FWIN],
                                  qdst, waits=[("vQADD", vq)],
                                  inc="wQ12_%d" % (win % 2))
                        else:
                            w = [("pA", 86 + 2 * win)]
                            if win == 0 and wname == "w1" and g == 0:
                                w.append(("wQ12_0", wQ_A["t0"]))
                                w.append(("wQ12_1", wQ_A["t1"]))
                            slot = "dW12_%s%d%d" % (wname[1], g, win % 2)
                            S.dma("sp", qdst,
                                  qsdst[wname].rearrange("(j p) f -> p j f", p=128)[
                                      :, g * 8:(g + 1) * 8,
                                      win * FWIN:(win + 1) * FWIN],
                                  waits=w, inc=slot)
                vq_of[win] = S.c("vQADD")
                dw_of[win] = [("dW12_%s%d%d" % (wn, gg, win % 2),
                               S.c("dW12_%s%d%d" % (wn, gg, win % 2)))
                              for wn in ("1", "2") for gg in range(2)]
                if h == 0 and win % 2 == 1:
                    gamma3_batch(4, g3slots)

            def a_mm_step(fc):
                par = fc % 2
                if h == 0:
                    wmm = [("vQADD", vq_of[fc]), ("dX1CM", dx1)]
                else:
                    wmm = [("dX1CM", dx1)] + dw_of[fc]
                if fc >= 2:
                    wmm += [("aSIG", sig_of[fc - 2]), ("vHID", hid_of[fc - 2])]
                elif h == 1:
                    wmm += [("aSIG", wQ_A["sig"][fc]), ("vHID", wQ_A["hid"][fc]),
                            ("vR2EV", wQ_A["r2ev"])]

                def a_mm(wname_i, bank0, fci):
                    qsb = qb[wname_i][fci % 2]
                    def f(e):
                        ii = None
                        for n in range(2):
                            for k in range(NKD):
                                ii = e.matmul(ps8[:, bank0 + n, :],
                                              qsb[:, k, 0:128],
                                              x1cm[:, k, n * 512:(n + 1) * 512],
                                              start=(k == 0),
                                              stop=(k == NKD - 1))
                        return ii
                    return f
                b0 = 0 if par == 0 else 4
                S.op("pe", a_mm("w1", b0, fc), waits=wmm, inc="pA")
                S.op("pe", a_mm("w2", b0 + 2, fc), inc="pA")
                pa_of[fc] = S.c("pA")

            def a_evac(fc):
                b0 = 0 if fc % 2 == 0 else 4
                S.op("act", (lambda bk, ib: (lambda e: e.activation(
                    sigb[ib].rearrange("p (a b) -> p a b", a=2),
                    ps8[:, bk:bk + 2, :], AF.Sigmoid, scale=1.0)))(b0, fc % 2),
                    waits=[("pA", pa_of[fc])], inc="aSIG")
                sig_of[fc] = S.c("aSIG")
                hwait = [("aSIG", sig_of[fc])]
                if h == 1 and fc == 0:
                    hwait.append(("pBf", 88))
                S.op("dve", (lambda bk, fci, ib: (lambda e: e.scalar_tensor_tensor(
                    hidden[:, fci, :].rearrange("p (a b) -> p a b", a=2),
                    ps8[:, bk + 2:bk + 4, :], 0.5,
                    sigb[ib].rearrange("p (a b) -> p a b", a=2),
                    ALU.mult, ALU.mult)))(b0, fc, fc % 2),
                    waits=hwait, inc="vHID")
                hid_of[fc] = S.c("vHID")

            for win in range(NWIN + 2):
                if win < NWIN:
                    a_quant(win)
                if win >= 1 and win - 1 < NWIN:
                    a_mm_step(win - 1)
                if win >= 2:
                    a_evac(win - 2)
            if h == 0:
                wQ_A["t0"] = S.c("wQ12_0")
                wQ_A["t1"] = S.c("wQ12_1")
                wQ_A["sig"] = (sig_of[NWIN - 2], sig_of[NWIN - 1])
                wQ_A["hid"] = (hid_of[NWIN - 2], hid_of[NWIN - 1])
                gamma3_batch(88, g3slots)
                gamma_final(2, extra=[("vHID", S.c("vHID")),
                                      ("aSIG", S.c("aSIG"))])

            if sub < 2:
                return

            vhid_all = S.c("vHID")
            for dg in range(4):
                for grp in range(22):
                    if h == 0:
                        src = w3T.rearrange("(q p) d -> p q d", p=128)
                        stg = w3stage[grp % 2]
                        slot3 = "dW3s%d" % (grp % 2)
                        S.dma("sp", stg[:],
                              src[:, grp * 2:(grp + 1) * 2,
                                  dg * 512:(dg + 1) * 512],
                              waits=[("aSGN", max(0, S.c("aSGN") - 1))], inc=slot3)
                        sc = gscale[:, 2:3]
                        qdst = q3b[grp % 3]
                        sgn_w = [(slot3, S.c(slot3)),
                                 ("pBf", max(0, S.c("pBf") - 2)),
                                 ("wQ3_%d" % (grp % 3),
                                  S.c("wQ3_%d" % (grp % 3)))]
                        if dg == 0 and grp == 0:
                            sgn_w.append(("vGS", S.c("vGS")))
                        tq2 = tmpq.rearrange("p a j b -> p a (j b)").rearrange(
                            "p a (q b) -> p a q b", q=2)
                        S.op("dve", (lambda st: (lambda e: e.tensor_scalar(
                            tq2[:, 0, :, :], st[:], gscale[:, 5:6],
                            2.0, ALU.is_gt, ALU.mult)))(stg),
                            waits=sgn_w)
                        S.op("dve", (lambda st: (lambda e: e.tensor_scalar(
                            tq2[:, 1, :, :], st[:], gscale[:, 8:9],
                            2.0, ALU.is_lt, ALU.mult)))(stg),
                            inc="aSGN")
                        S.op("dve", (lambda qd: (lambda e: e.tensor_tensor(
                            qd[:], tq2[:, 0, :, :], tq2[:, 1, :, :],
                            ALU.subtract)))(qdst),
                            inc="vQADD")
                        vq = S.c("vQADD")
                        S.dma("pool",
                              q3s.rearrange("(q p) d -> p q d", p=128)[
                                  :, grp * 2:(grp + 1) * 2,
                                  dg * 512:(dg + 1) * 512],
                              qdst[:], waits=[("vQADD", vq)], inc="wQ3_%d" % (grp % 3))
                    else:
                        wq3 = [("pBf", max(0, S.c("pBf") - 2))]
                        if dg == 0 and grp == 0:
                            for qsem in ("wQ3_0", "wQ3_1", "wQ3_2"):
                                wq3.append((qsem, S.c(qsem)))
                        slotq = "dW3q%d" % (grp % 3)
                        S.dma("sp", q3b[grp % 3][:],
                              q3s.rearrange("(q p) d -> p q d", p=128)[
                                  :, grp * 2:(grp + 1) * 2,
                                  dg * 512:(dg + 1) * 512],
                              waits=wq3, inc=slotq)
                    if h == 0:
                        wmm = [("vQADD", S.c("vQADD"))]
                    else:
                        wmm = [(slotq, S.c(slotq))]
                    if grp == 0 and dg == 0:
                        wmm.append(("vHID", vhid_all))
                    if grp == 0 and dg > 0:
                        wmm.append(("vR2EV", S.c("vR2EV")))

                    def b_mm(grpi):
                        def f(e):
                            ii = None
                            for q in range(2):
                                ft = grpi * 2 + q
                                for s in range(8):
                                    ii = e.matmul(ps8[:, s, :],
                                                  hidden[:, ft, s * 128:(s + 1) * 128],
                                                  q3b[grpi % 3][:, q, :],
                                                  start=(ft == 0),
                                                  stop=(ft == NFC - 1))
                            return ii
                        return f
                    S.op("pe", b_mm(grp), waits=wmm, inc="pBf")
                pb = S.c("pBf")

                for s in range(8):
                    tt = h * 8 + s
                    trow = h * TH + s * 128
                    li = fst["x1tmld"]
                    S.dma("sp", x1tmld[li % 2][:],
                          x1tm_sp[trow:trow + 128, dg * 512:(dg + 1) * 512],
                          waits=[("wXT0", 128), ("wXT1", 128),
                                 ("vR2EV", max(0, li - 1))],
                          inc="dXT%d" % (li % 2))
                    fst["x1tmld"] += 1
                    ri = fst["r2ev"]
                    S.op("dve", (lambda ss, lii, rii: (lambda e: e.scalar_tensor_tensor(
                        r2t[rii % 2][:], ps8[:, ss, :], 1.0, x1tmld[lii % 2][:],
                        ALU.mult, ALU.add, accum_out=parts[:, 0:1])))(s, li, ri),
                        waits=[("pBf", pb),
                               ("dXT%d" % (li % 2), S.c("dXT%d" % (li % 2))),
                               ("wR2_%d" % (ri % 2), max(0, 16 * (ri // 2)))],
                        inc="vR2S")
                    S.op("act", (lambda rii: (lambda e: e.activation(
                        junk512[:], r2t[rii % 2][:], AF.Square,
                        accum_out=parts[:, 1:2])))(ri),
                        waits=[("vR2S", S.c("vR2S")),
                               ("vR2EV", max(0, ri))], inc="aSQ")
                    asq2 = S.c("aSQ")
                    if dg == 0:
                        S.op("dve", (lambda ttt: (lambda e: e.tensor_copy(
                            sums[:, ttt:ttt + 1], parts[:, 0:1])))(tt),
                            waits=[("aSQ", asq2)])
                        S.op("dve", (lambda ttt: (lambda e: e.tensor_copy(
                            sumsq[:, ttt:ttt + 1], parts[:, 1:2])))(tt),
                            inc="vR2EV")
                    else:
                        S.op("dve", (lambda ttt: (lambda e: e.tensor_tensor(
                            sums[:, ttt:ttt + 1], sums[:, ttt:ttt + 1],
                            parts[:, 0:1], ALU.add)))(tt),
                            waits=[("aSQ", asq2)])
                        S.op("dve", (lambda ttt: (lambda e: e.tensor_tensor(
                            sumsq[:, ttt:ttt + 1], sumsq[:, ttt:ttt + 1],
                            parts[:, 1:2], ALU.add)))(tt),
                            inc="vR2EV")
                    vre = S.c("vR2EV")
                    S.dma("pool", r2_sp[trow:trow + 128, dg * 512:(dg + 1) * 512],
                          r2t[ri % 2][:], waits=[("vR2EV", vre)],
                          inc="wR2_%d" % (ri % 2))
                    fst["r2ev"] += 1

                    if dg == 3:
                        S.op("dve", (lambda ttt: (lambda e: e.tensor_scalar(
                            negm[:, ttt:ttt + 1], sums[:, ttt:ttt + 1],
                            -1.0 / D, None, ALU.mult)))(tt),
                            waits=[("vR2EV", vre), ("aRSTD", S.c("aRSTD"))])
                        S.op("dve", (lambda ttt: (lambda e: e.tensor_scalar(
                            lnt[:, 1:2], negm[:, ttt:ttt + 1],
                            negm[:, ttt:ttt + 1], None, ALU.mult)))(tt))
                        S.op("dve", (lambda ttt: (lambda e: e.scalar_tensor_tensor(
                            lnt[:, 2:3], sumsq[:, ttt:ttt + 1], 1.0 / D,
                            lnt[:, 1:2], ALU.mult, ALU.subtract)))(tt),
                            inc="vNRM")
                        vn = S.c("vNRM")
                        S.op("act", lambda e: e.activation(
                            lnt[:, 2:3], lnt[:, 2:3], AF.Ln,
                            bias=consts[:, 3:4], scale=1.0),
                            waits=[("vNRM", vn)])
                        S.op("act", (lambda ttt: (lambda e: e.activation(
                            rstd[:, ttt:ttt + 1], lnt[:, 2:3], AF.Exp,
                            scale=-0.5)))(tt),
                            inc="aRSTD")
                        ars = S.c("aRSTD")
                        S.dma("pool", r2full[0][:], r2_sp[trow:trow + 128, :],
                              waits=[("wR2_0", S.c("wR2_0")),
                                     ("wR2_1", S.c("wR2_1")),
                                     ("gG3", 88)], inc="dR2RD")
                        drr = S.c("dR2RD")
                        S.op("dve", (lambda ttt: (lambda e: e.tensor_scalar(
                            outst[0][:], r2full[0][:], negm[:, ttt:ttt + 1],
                            rstd[:, ttt:ttt + 1], ALU.add, ALU.mult)))(tt),
                            waits=[("aRSTD", ars), ("dR2RD", drr),
                                   ("dOUT", max(0, 16 * tt))], inc="vNRM")
                        vn2 = S.c("vNRM")
                        S.dma("pool", out_d[trow:trow + 128, :], outst[0][:],
                              waits=[("vNRM", vn2)], inc="dOUT")

        if STAGE >= 3:
            ffn_half(0, sub=(0 if STAGE == 30 else (1 if STAGE == 31 else 2)))
        wQ_A["r2ev"] = S.c("vR2EV")
        if STAGE >= 4:
            ffn_half(1, sub=(0 if STAGE == 41 else (1 if STAGE == 42 else (3 if STAGE == 43 else 2))))
        if STAGE != 4 and STAGE < 99:
            wtail = [(n, S.c(n)) for n in (
                "vGS", "wXT0", "wXT1", "wXC", "pTP", "vTPC", "wQ12_0",
                "wQ12_1", "wQ3_0", "wQ3_1", "wQ3_2", "dHS0", "dHS1", "dHS2",
                "dHSTM0", "dHSTM1", "vSCAN",
                "aSIG", "vHID", "wR2_0", "wR2_1", "dOUT", "gG", "gGa",
                "pIPx", "pIPz", "pWOUT")]
            S.dma("sp", out_d[0:128, :], outst[0][:], waits=wtail, inc="dOUT")

        nout = S.c("dOUT")
        S.op("act", lambda e: e.nop(), waits=[("dOUT", nout)])
        S.op("sp", lambda e: e.nop(), waits=[("dOUT", nout)])
        S.op("dve", lambda e: e.nop(), waits=[("dOUT", nout)])

        self_sems = {"dve": ctx.enter_context(nc.semaphore("selfDVE")),
                     "act": ctx.enter_context(nc.semaphore("selfACT")),
                     "pool": ctx.enter_context(nc.semaphore("selfPOOL"))}
        block = ctx.enter_context(nc.Block())
        S.emit(block, self_sems)

    build_program.last_sched = S
    return nc


_NC_CACHE = {}


def kernel(L0_attn_hidden_states, L0_attn_W_in_x, L0_attn_W_in_z, L0_attn_conv_w,
           L0_attn_conv_b, L0_attn_W_param, L0_attn_W_dt, L0_attn_b_dt,
           L0_attn_A_log, L0_attn_D, L0_attn_W_out, L0_ffn_w1, L0_ffn_w2, L0_ffn_w3):
    f32 = np.float32
    hs = np.ascontiguousarray(np.asarray(L0_attn_hidden_states, dtype=f32))
    c = lambda a: np.ascontiguousarray(np.asarray(a, dtype=f32))

    winxT = c(np.asarray(L0_attn_W_in_x, dtype=f32).T)
    winzT = c(np.asarray(L0_attn_W_in_z, dtype=f32).T)
    wparamT = c(np.asarray(L0_attn_W_param, dtype=f32).T)
    wdtT = c(np.repeat(np.asarray(L0_attn_W_dt, dtype=f32), 16, axis=0).T)
    woutT = c(np.asarray(L0_attn_W_out, dtype=f32).T)
    w1T = c(np.asarray(L0_ffn_w1, dtype=f32).T)
    w2T = c(np.asarray(L0_ffn_w2, dtype=f32).T)
    w3T = c(np.asarray(L0_ffn_w3, dtype=f32).T)
    convw = c(np.asarray(L0_attn_conv_w, dtype=f32).reshape(4, 128, 4)
              .transpose(1, 0, 2).reshape(128, 16))
    convb = c(np.asarray(L0_attn_conv_b, dtype=f32).reshape(4, 128).T)
    bdt = c(np.repeat(np.asarray(L0_attn_b_dt, dtype=f32), 16).reshape(4, 128).T)
    aflat = c(np.asarray(L0_attn_A_log, dtype=f32).reshape(4, 128).T)
    dvec = c(np.asarray(L0_attn_D, dtype=f32).reshape(4, 128).T)

    if "nc" not in _NC_CACHE:
        _NC_CACHE["nc"] = build_program()
    nc = _NC_CACHE["nc"]
    in_maps = []
    for core in range(8):
        b, h = core // 2, core % 2
        hs_tm_c = np.ascontiguousarray(hs[b, h * T:(h + 1) * T, :])
        in_maps.append({
            "hs_tm": hs_tm_c,
            "hs_cm": np.ascontiguousarray(hs_tm_c.T),
            "aux_cm": np.ascontiguousarray(hs[b, 0:T, :].T),
            "winxT": winxT, "winzT": winzT, "wparamT": wparamT, "wdtT": wdtT,
            "woutT": woutT, "w1T": w1T, "w2T": w2T, "w3T": w3T,
            "convw": convw, "convb": convb, "bdt": bdt, "aflat": aflat,
            "dvec": dvec,
            "flag": np.full((128, 1), float(h), dtype=f32),
        })
    res = run_bass_kernel_spmd(nc, in_maps, core_ids=list(range(8)))
    out = np.empty((4, 4096, D), dtype=f32)
    for core in range(8):
        b, h = core // 2, core % 2
        out[b, h * T:(h + 1) * T, :] = np.asarray(res.results[core]["out"])
    return out


if __name__ == "__main__":
    import jax
    _cpu = jax.devices("cpu")[0]
    with jax.default_device(_cpu):
        import reference as _r
        ins = {k: np.asarray(v) for k, v in _r.setup_inputs(0).items()}
    got = kernel(**ins)
    with jax.default_device(_cpu):
        want = np.asarray(_r.reference(**{k: jax.device_put(v, _cpu) for k, v in ins.items()}))
    num = float(np.mean((got - want) ** 2))
    den = float(np.mean(want ** 2))
    print("resid var ratio:", num / den)


# version 26
# speedup vs baseline: 1.0333x; 1.0333x over previous
import os
import numpy as np
from contextlib import ExitStack

import concourse.bass as bass
import concourse.mybir as mybir
from concourse.bass_utils import run_bass_kernel_spmd

F32 = mybir.dt.float32
BF16 = mybir.dt.bfloat16
AF = mybir.ActivationFunctionType
ALU = mybir.AluOpType
AX = mybir.AxisListType

D = 2048
T = 2048
TC = 512
NCH = T // TC
DI = 512
NCT = DI // 128
NKD = D // 128
NPAR = 9
NPAR_AUX = 5
FH = 5632
NFC = FH // 128
TH = 1024
FWIN = 128
NWIN = FH // FWIN
EPS_LN = 1e-5
EPS_Q = 1e-8
NW_ELEM = float(FH * D)

ENGINES = ("pe", "dve", "act", "pool", "sp")


class Sched:
    def __init__(self, nc):
        self.nc = nc
        self.streams = {e: [] for e in ENGINES}
        self.counts = {}
        self.handles = {}

    def sem(self, ctx, name):
        h = ctx.enter_context(self.nc.semaphore(name))
        self.counts[name] = 0
        self.handles[name] = h

    def c(self, name):
        return self.counts[name]

    def op(self, eng, fn, waits=(), inc=None, amt=1, relax=1):
        w = [(self.handles[s], v) for (s, v) in waits if v > 0]
        if not hasattr(self, "debug"):
            self.debug = []
        self.debug.append((eng, [(s, v) for (s, v) in waits if v > 0], inc, amt))
        if inc is not None:
            self.counts[inc] += amt
            self.streams[eng].append((w, fn, self.handles[inc], amt, relax))
        else:
            self.streams[eng].append((w, fn, None, 0, relax))

    def dma(self, queue, out, in_, waits=(), inc=None):
        self.op(queue, (lambda o, i: (lambda e: e.dma_start(o, i)))(out, in_),
                waits=waits, inc=inc, amt=16)

    def dmat(self, queue, out, in_, waits=(), inc=None):
        self.op(queue,
                (lambda o, i: (lambda e: e.dma_start_transpose(o, i)))(out, in_),
                waits=waits, inc=inc, amt=16)

    def emit(self, block, self_sems):
        nc = self.nc
        eng_obj = {"pe": nc.tensor, "dve": nc.vector, "act": nc.scalar,
                   "pool": nc.gpsimd, "sp": nc.sync}

        class SerialProxy:
            _SKIP = {"wait_ge", "dma_start", "dma_start_transpose", "nop",
                     "engine_nop", "sem_inc", "sem_clear"}

            def __init__(self, eng, sem):
                self._eng = eng
                self._sem = sem
                self._n = 0
                self._relax = 1
                self._base = 0

            def __getattr__(self, name):
                real = getattr(self._eng, name)
                if name in self._SKIP or not callable(real):
                    return real

                def wrap(*a, **kw):
                    tgt = max(self._base, self._n - (self._relax - 1))
                    if tgt > 0:
                        self._eng.wait_ge(self._sem, tgt)
                    instr = real(*a, **kw)
                    if instr is not None:
                        instr.then_inc(self._sem, 1)
                        self._n += 1
                    return instr
                return wrap

        def mk(ename):
            def body(_):
                raw = eng_obj[ename]
                e = raw
                prox = None
                if ename in self_sems:
                    prox = SerialProxy(raw, self_sems[ename])
                    e = prox
                for (waits, fn, inc_h, amt, relax) in self.streams[ename]:
                    for (h, v) in waits:
                        raw.wait_ge(h, v)
                    n0 = prox._n if prox else 0
                    if prox is not None:
                        prox._relax = relax
                        prox._base = prox._n
                    instr = fn(e)
                    if prox is not None:
                        prox._relax = 1
                    if inc_h is not None:
                        assert instr is not None, f"{ename} op with inc returned None"
                        if prox is not None and prox._n > n0:
                            raw.wait_ge(self_sems[ename], prox._n)
                            raw.sem_inc(inc_h, amt)
                        else:
                            instr.then_inc(inc_h, amt)
            return body

        block.tensor(mk("pe"))
        block.vector(mk("dve"))
        block.scalar(mk("act"))
        block.gpsimd(mk("pool"))
        block.sync(mk("sp"))


def build_program():
    STAGE = int(os.environ.get("KSTAGE", "4"))
    nc = bass.Bass("TRN2", target_bir_lowering=False)

    def din(name, shape, dtype=F32):
        return nc.dram_tensor(name, shape, dtype, kind="ExternalInput").ap()

    hs_tm = din("hs_tm", [T, D])
    hs_cm = din("hs_cm", [D, T])
    aux_cm = din("aux_cm", [D, T])
    winxT = din("winxT", [D, DI])
    winzT = din("winzT", [D, DI])
    wparamT = din("wparamT", [DI, 1152])
    wdtT = din("wdtT", [128, DI])
    woutT = din("woutT", [DI, D])
    w1T = din("w1T", [D, FH])
    w2T = din("w2T", [D, FH])
    w3T = din("w3T", [FH, D])
    convw = din("convw", [128, 16])
    convb = din("convb", [128, NCT])
    bdt = din("bdt", [128, NCT])
    aflat = din("aflat", [128, NCT])
    dvecd = din("dvec", [128, NCT])
    flag_d = din("flag", [128, 1])

    out_d = nc.dram_tensor("out", [T, D], F32, kind="ExternalOutput").ap()

    _dbg = "ExternalOutput" if os.environ.get("KDBG") else "Internal"
    x1tm_sp = nc.dram_tensor("x1tm_sp", [T, D], F32, kind=_dbg).ap()
    x1cm_sp = nc.dram_tensor("x1cm_sp", [NKD, 128, T], BF16, kind=_dbg).ap()
    r2_sp = nc.dram_tensor("r2_sp", [T, D], F32, kind=_dbg).ap()
    q1s = nc.dram_tensor("q1s", [D, FH], BF16, kind=_dbg).ap()
    q2s = nc.dram_tensor("q2s", [D, FH], BF16, kind=_dbg).ap()
    q3s = nc.dram_tensor("q3s", [FH, D], BF16, kind=_dbg).ap()

    S = Sched(nc)
    ctx = ExitStack()
    with ctx:
        sb = lambda name, shape, dt=F32: ctx.enter_context(nc.sbuf_tensor(name, shape, dt))

        consts = sb("consts", [128, 4])
        ones_col = sb("ones_col", [128, 1])
        ones_row = sb("ones_row", [1, 128])
        gtmp = sb("gtmp", [1, 1])
        gscale = sb("gscale", [128, 9])
        smallw = sb("smallw", [128, 33])
        cw = smallw[:, 0:16]
        cb = smallw[:, 16:20]
        bd = smallw[:, 20:24]
        afl = smallw[:, 24:28]
        dv = smallw[:, 28:32]
        fl = smallw[:, 32:33]
        a_neg = sb("a_neg", [128, NCT])
        state_aux = sb("state_aux", [128, NCT])
        state_own = sb("state_own", [128, NCT])
        sums = sb("sums", [128, 16])
        sumsq = sb("sumsq", [128, 16])
        negm = sb("negm", [128, 16])
        rstd = sb("rstd", [128, 16])
        rstdh = sb("rstdh", [128, 16])
        lnt = sb("lnt", [128, 4])
        lnv = sb("lnv", [128, 8])
        parts = sb("parts", [128, 2])
        gcols = sb("gcols", [128, 344])

        ident = sb("ident", [128, 128], BF16)
        ps8 = ctx.enter_context(nc.psum_tensor("ps8", [128, 8, 512], F32))
        ptp = ps8[:, 0:2, :].bitcast(BF16)

        mctx = ExitStack()
        msb = lambda name, shape, dt=F32: mctx.enter_context(nc.sbuf_tensor(name, shape, dt))
        winx = msb("winx", [128, NKD, DI], BF16)
        winz = msb("winz", [128, NKD, DI], BF16)
        wpar = msb("wpar", [128, NCT, 1152], BF16)
        wdt = msb("wdt", [128, DI], BF16)
        wout = msb("wout", [128, NCT, D], BF16)
        hsb = msb("hsb", [128, 3, 8, TC], BF16)
        x_sb = msb("x_sb", [128, NCT, TC + 3])
        xtail = msb("xtail", [128, NCT, 3])
        xact = msb("xact", [128, NCT, TC])
        xactb = msb("xactb", [128, NCT, TC], BF16)
        sz = msb("sz", [128, NCT, TC])
        bmat = msb("bmat", [128, NCT, TC])
        cmat = msb("cmat", [128, NCT, TC])
        dtb = msb("dtb", [128, TC], BF16)
        delta = msb("delta", [128, NCT, TC])
        a_sb = msb("a_sb", [128, NCT, TC])
        h_sb = msb("h_sb", [128, NCT, TC])
        y3 = msb("y3", [128, NCT, TC], BF16)
        r1 = msb("r1", [128, 2, D])
        x1bf = msb("x1bf", [128, 4, D], BF16)
        tpst = msb("tpst", [128, D], BF16)
        junk2k = msb("junk2k", [128, D], BF16)
        gstage = [msb(f"gstage_{i}", [128, 704]) for i in range(3)]
        mctx.close()

        x1cm = sb("x1cm", [128, NKD, TH], BF16)
        hidden = sb("hidden", [128, NFC, TH], BF16)
        qb = {w: [sb(f"qb_{w}_{i}", [128, 16, FWIN], BF16) for i in range(2)]
              for w in ("w1", "w2")}
        sgnb = sb("sgnb", [128, 1024], BF16)
        sgnbA = sgnb.rearrange("p (a b) -> p a b", a=8)
        wstage = [sb(f"wstage_{i}", [128, 8, FWIN]) for i in range(4)]
        tmpq = sb("tmpq", [128, 2, 8, FWIN], BF16)
        sigb = [sb(f"sigb_{i}", [128, TH], BF16) for i in range(2)]
        q3b = [sb(f"q3b_{i}", [128, 2, 512], BF16) for i in range(3)]
        w3stage = [sb(f"w3stage_{i}", [128, 2, 512]) for i in range(2)]
        x1tmld = [sb(f"x1tmld_{i}", [128, 512]) for i in range(2)]

        r2t = [sb(f"r2t_{i}", [128, 512]) for i in range(2)]
        partsB = sb("partsB", [128, 16])
        junk512 = sb("junk512", [128, 512])
        r2full = [sb("r2full_0", [128, D])]
        outst = [sb("outst_0", [128, D])]
        g3slots = [r2full[0][:, 0:1024], r2full[0][:, 1024:2048],
                   outst[0][:, 0:1024], outst[0][:, 1024:2048]]

        for name in ("dHS0", "dHS1", "dHS2", "dHSTM0", "dHSTM1", "dWM",
                     "dWX", "dWZ", "dWP", "dWO", "dWD",
                     "dG0", "dG1", "dG3", "dX1CM",
                     "dW12_100", "dW12_101", "dW12_110", "dW12_111",
                     "dW12_200", "dW12_201", "dW12_210", "dW12_211",
                     "dW3s0", "dW3s1", "dW3s2", "dW3q0", "dW3q1", "dW3q2",
                     "dXT0", "dXT1",
                     "dR2RD", "dOUT", "wXT0", "wXT1", "wXC", "pTP", "vTPC",
                     "wQ12_0", "wQ12_1", "wQ3_0", "wQ3_1", "wQ3_2",
                     "wR2_0", "wR2_1",
                     "pIPx", "pIPz", "pPAR", "pDT", "pWOUT", "pGM",
                     "pA", "pB", "pBf",
                     "gG", "gGa", "gG3", "dG30", "dG31", "dG32", "dG33", "dG2", "vGS",
                     "vEVX", "vXB", "vPAR", "vB", "vSCAN", "vY3", "vR1", "vX1",
                     "vQADD", "vHID", "vR2S", "vR2EV", "vNRM",
                     "aZ", "aXA", "aDELTA", "aAA", "aSQ", "aRSTD", "aSGN", "aSIG",
                     "init"):
            S.sem(ctx, name)

        def init_fn(e):
            e.memset(ident[:], 0.0)
            e.affine_select(out=ident[:], in_=ident[:], compare_op=ALU.not_equal,
                            fill=1.0, base=0, pattern=[[-1, 128]], channel_multiplier=1)
            e.memset(consts[:, 0:1], -0.5)
            e.memset(consts[:, 1:2], 0.5)
            e.memset(consts[:, 2:3], 1.0)
            e.memset(consts[:, 3:4], EPS_LN)
            e.memset(ones_col[:], 1.0)
            return e.memset(ones_row[:], 1.0)
        S.op("pool", init_fn, inc="init")

        S.dma("sp", smallw[:, 0:16], convw[:], inc="dWM")
        S.dma("sp", cb, convb[:], inc="dWM")
        S.dma("sp", bd, bdt[:], inc="dWM")
        S.dma("sp", afl, aflat[:], inc="dWM")
        S.dma("sp", dv, dvecd[:], inc="dWM")
        S.dma("sp", fl, flag_d[:], inc="dWM")
        n_wm = S.c("dWM")

        winx_r = winxT.rearrange("(j p) f -> p j f", p=128)
        for j0 in range(0, NKD, 4):
            S.dma("pool", winx[:, j0:j0 + 4, :], winx_r[:, j0:j0 + 4, :],
                  waits=[("init", 1)], inc="dWX")
        n_winx = S.c("dWX")

        def emit_hs_half(h):
            g = h // 2
            j = h % 2
            sl = h % 3
            src = hs_cm if g >= 4 else aux_cm
            ci = g % 4
            src_r = src.rearrange("(k p) t -> p k t", p=128)
            w = []
            hprev = h - 3
            if hprev >= 0:
                gp = hprev // 2
                if gp <= 3:
                    w.append(("pIPx", gp + 1))
                else:
                    w.append(("pIPz", gp - 3))
            for q in range(2):
                k0 = 8 * j + 4 * q
                S.dma("pool", hsb[:, sl, 4 * q:4 * q + 4, :],
                      src_r[:, k0:k0 + 4, ci * TC:(ci + 1) * TC],
                      waits=(w if q == 0 else []), inc="dHS%d" % sl)

        gma = {"dma": 0, "red": 0, "dma3": 0, "red3": 0}
        w3r = w3T.rearrange("(j p) d -> p j d", p=128)
        gtiles = []
        for wi, src in ((0, w1T), (1, w2T)):
            for blk in range(16):
                for q in range(8):
                    gtiles.append((src[blk * 128:(blk + 1) * 128,
                                       q * 704:(q + 1) * 704], 704))
        NGT = len(gtiles)

        def g_eng(gi):
            return "dve" if (gi // 8) % 2 == 0 else "act"

        def g_cnt(gi):
            full, rem = (gi + 1) // 16, (gi + 1) % 16
            if g_eng(gi) == "dve":
                return full * 8 + min(rem, 8)
            return full * 8 + max(0, rem - 8)

        def g_wait(gi):
            return (("gG" if g_eng(gi) == "dve" else "gGa"), g_cnt(gi))

        g3tiles = []
        for j in range(44):
            for k in range(2):
                g3tiles.append((w3r[:, j, k * 1024:(k + 1) * 1024], 1024))

        def gamma_dma(n):
            for _ in range(n):
                gi = gma["dma"]
                if gi >= NGT:
                    return
                tile, width = gtiles[gi]
                wv = [g_wait(gi - 3)] if gi >= 3 else []
                S.dma("sp", gstage[gi % 3][:, 0:width], tile,
                      waits=wv, inc="dG%d" % (gi % 3))
                gma["dma"] += 1

        def gamma_reduce(n, eng="dve"):
            for _ in range(n):
                gi = gma["red"]
                if gi >= NGT:
                    return
                width = gtiles[gi][1]
                if g_eng(gi) == "dve":
                    S.op("dve",
                         (lambda g, cc, wd: (lambda e: e.tensor_reduce(
                             gcols[:, cc:cc + 1], gstage[g % 3][:, 0:wd], AX.X,
                             ALU.add, apply_absolute_value=True)))(gi, gi, width),
                         waits=[("dG%d" % (gi % 3), 16 * (gi // 3 + 1))], inc="gG")
                else:
                    S.op("act",
                         (lambda g, cc, wd: (lambda e: e.activation(
                             junk2k[:, 0:wd], gstage[g % 3][:, 0:wd], AF.Abs,
                             accum_out=gcols[:, cc:cc + 1])))(gi, gi, width),
                         waits=[("dG%d" % (gi % 3), 16 * (gi // 3 + 1))], inc="gGa")
                gma["red"] += 1

        def gamma3_batch(n, slots):
            for _ in range(n):
                gi = gma["dma3"]
                if gi >= 88:
                    break
                tile, width = g3tiles[gi]
                S.dma("sp", slots[gi % 4][:, 0:width],
                      tile, waits=[("gG3", max(0, gi - 3))],
                      inc="dG3%d" % (gi % 4))
                gma["dma3"] += 1
            batch = []
            for _ in range(n):
                gi = gma["red3"]
                if gi >= 88:
                    break
                batch.append(gi)
                gma["red3"] += 1
            if batch:
                bw = [("dG3%d" % (gi % 4), 16 * (gi // 4 + 1)) for gi in batch]

                def red_fn(tiles):
                    def f(e):
                        ii = None
                        for gi in tiles:
                            wd = g3tiles[gi][1]
                            ii = e.tensor_reduce(
                                gcols[:, 256 + gi:257 + gi],
                                slots[gi % 4][:, 0:wd], AX.X,
                                ALU.add, apply_absolute_value=True)
                        return ii
                    return f
                S.op("dve", red_fn(list(batch)), waits=bw,
                     inc="gG3", amt=len(batch), relax=4)

        def gamma_final(wi, extra=()):
            g0, g1 = ((0, 128), (128, 256), (256, 344))[wi]
            if wi == 2:
                gw = [("gG3", 88)]
            else:
                gw = [("gG", g_cnt(g1 - 1) if g_eng(g1 - 1) == "dve" else g1 // 2),
                      ("gGa", g1 // 2)]
                gw = [("gG", g1 // 2), ("gGa", g1 // 2)]
            S.op("dve", (lambda a, b: (lambda e: e.tensor_reduce(
                lnt[:, 0:1], gcols[:, a:b], AX.X, ALU.add)))(g0, g1),
                 waits=gw + [("pGM", S.c("pGM"))] + list(extra), inc="vGS")
            vgs = S.c("vGS")
            S.op("pe", lambda e: e.matmul(ps8[:1, 7, 0:1], lnt[:, 0:1], ones_col[:],
                                          start=True, stop=True),
                 waits=[("vGS", vgs), ("aDELTA", S.c("aDELTA")),
                        ("vR1", S.c("vR1"))], inc="pGM")
            pg = S.c("pGM")
            S.op("dve", lambda e: e.tensor_copy(gtmp[:], ps8[:1, 7, 0:1]),
                 waits=[("pGM", pg)], inc="vGS")
            vgs2 = S.c("vGS")
            S.op("pe", lambda e: e.matmul(ps8[:, 7, 1:2], ones_row[:], gtmp[:],
                                          start=True, stop=True),
                 waits=[("vGS", vgs2)], inc="pGM")
            pg2 = S.c("pGM")
            S.op("dve", (lambda w: (lambda e: e.tensor_scalar(
                gscale[:, w:w + 1], ps8[:, 7, 1:2], 1.0 / NW_ELEM, EPS_Q,
                ALU.mult, ALU.add)))(wi),
                 waits=[("pGM", pg2)], inc="vGS")
            S.op("dve", (lambda w: (lambda e: e.tensor_scalar(
                gscale[:, 3 + w:4 + w], gscale[:, w:w + 1], 0.5, None,
                ALU.mult)))(wi), inc="vGS")
            S.op("dve", (lambda w: (lambda e: e.tensor_scalar(
                gscale[:, 6 + w:7 + w], gscale[:, w:w + 1], -0.5, None,
                ALU.mult)))(wi), inc="vGS")
            S.op("dve", (lambda w: (lambda e: e.reciprocal(
                gscale[:, w:w + 1], gscale[:, w:w + 1])))(wi), inc="vGS")

        S.op("act", lambda e: e.activation(a_neg[:], afl, AF.Exp),
             waits=[("dWM", n_wm)], inc="aAA")
        S.op("dve", lambda e: e.tensor_scalar_mul(a_neg[:], a_neg[:], -1.0),
             waits=[("aAA", 1)], inc="vB")

        def emit_rest_weights():
            for (dst, src, nj, sem) in ((winz, winzT, NKD, "dWZ"),
                                        (wpar, wparamT, NCT, "dWP"),
                                        (wout, woutT, NCT, "dWO")):
                src_r = src.rearrange("(j p) f -> p j f", p=128)
                step = max(1, nj // 2)
                for j0 in range(0, nj, step):
                    j1 = min(nj, j0 + step)
                    S.dma("pool", dst[:, j0:j1, :], src_r[:, j0:j1, :],
                          inc=sem)
            S.dma("pool", wdt[:], wdtT[:], inc="dWD")

        mst = {"tsub": 0}
        vx1_of = {}

        def own(g):
            return g >= 4

        def emit_ipx(g):
            ci = g % 4
            w = [("dHS%d" % ((2 * g) % 3), 32 * ((2 * g) // 3 + 1)),
                 ("dHS%d" % ((2 * g + 1) % 3), 32 * ((2 * g + 1) // 3 + 1)),
                 ("vEVX", max(0, 2 * g - 1)),
                 ("vTPC", S.c("vTPC"))]
            if g == 0:
                w += [("dWX", 64)]

            def f(e):
                ii = None
                for k in range(NKD):
                    sl = (2 * g + k // 8) % 3
                    for c in range(NCT):
                        ii = e.matmul(ps8[:, c, :], winx[:, k, c * 128:(c + 1) * 128],
                                      hsb[:, sl, k % 8, :], start=(k == 0),
                                      stop=(k == NKD - 1))
                return ii
            S.op("pe", f, waits=w, inc="pIPx")

        def emit_ipz(g):
            w = [("vR1", S.c("vR1")), ("vPAR", S.c("vPAR")),
                 ("aDELTA", S.c("aDELTA")), ("aZ", S.c("aZ")),
                 ("vGS", S.c("vGS")), ("dWZ", 32)]

            def f(e):
                ii = None
                for k in range(NKD):
                    sl = (2 * g + k // 8) % 3
                    for c in range(NCT):
                        ii = e.matmul(ps8[:, 4 + c, :],
                                      winz[:, k, c * 128:(c + 1) * 128],
                                      hsb[:, sl, k % 8, :], start=(k == 0),
                                      stop=(k == NKD - 1))
                return ii
            S.op("pe", f, waits=w, inc="pIPz")

        def emit_evacx(g):
            ci = g % 4
            cvw = [("vEVX", 2 * g)] if g > 0 else []
            if g == 0:
                S.op("dve", lambda e: e.memset(x_sb[:, :, 0:3], 0.0))
            elif g == 4:
                S.op("dve", lambda e: e.tensor_scalar(
                    x_sb[:, :, 0:3], xtail[:], fl, None, ALU.mult), waits=cvw)
            else:
                S.op("dve", lambda e: e.tensor_copy(x_sb[:, :, 0:3],
                                                    x_sb[:, :, TC:TC + 3]),
                     waits=cvw)
            S.op("dve", lambda e: e.tensor_copy(x_sb[:, :, 3:TC + 3],
                                                ps8[:, 0:NCT, :]),
                 waits=[("pIPx", g + 1)], inc="vEVX")
            if g == 3:
                S.op("dve", lambda e: e.tensor_copy(xtail[:], x_sb[:, :, TC:TC + 3]))

        def emit_conv_silu_cast(g):
            def conv_fn(e):
                ii = None
                for c in range(NCT):
                    ii = e.tensor_scalar(xact[:, c, :], x_sb[:, c, 0:TC],
                                         cw[:, 4 * c:4 * c + 1], None, ALU.mult)
                for k in range(1, 4):
                    for c in range(NCT):
                        ii = e.scalar_tensor_tensor(
                            xact[:, c, :], x_sb[:, c, k:k + TC],
                            cw[:, 4 * c + k:4 * c + k + 1], xact[:, c, :],
                            ALU.mult, ALU.add)
                return ii
            cw_w = [("vB", S.c("vB")), ("vY3", S.c("vY3"))]
            if g == 0:
                cw_w.append(("dWM", n_wm))
            S.op("dve", conv_fn, waits=cw_w, inc="vEVX", relax=4)
            vcv = S.c("vEVX")

            def silu_fn(e):
                ii = None
                for c in range(NCT):
                    ii = e.activation(xact[:, c, :], xact[:, c, :], AF.Silu,
                                      bias=cb[:, c:c + 1], scale=1.0)
                return ii
            S.op("act", silu_fn, waits=[("vEVX", vcv)], inc="aXA", relax=4)
            S.op("dve", lambda e: e.tensor_copy(xactb[:], xact[:]),
                 waits=[("aXA", S.c("aXA")), ("pPAR", S.c("pPAR"))], inc="vXB")

        def emit_siluz(g):
            S.op("act", lambda e: e.activation(sz[:], ps8[:, 4:8, :], AF.Silu),
                 waits=[("pIPz", g - 3), ("vY3", S.c("vY3"))], inc="aZ")

        def emit_params(g):
            npar = NPAR if own(g) else NPAR_AUX
            vxb = S.c("vXB")
            vp0 = S.c("vPAR")

            def par_mm(mm):
                bank = 4 + mm % 4
                def f(e):
                    ii = None
                    for c in range(NCT):
                        ii = e.matmul(ps8[:, bank, :],
                                      wpar[:, c, mm * 128:(mm + 1) * 128],
                                      xactb[:, c, :], start=(c == 0),
                                      stop=(c == NCT - 1))
                    return ii
                return f

            for m in range(npar):
                w = []
                if m == 0:
                    w = [("vXB", vxb), ("pDT", S.c("pDT")),
                         ("vPAR", S.c("vPAR")), ("vR1", S.c("vR1")),
                         ("dWP", 32)]
                    if own(g):
                        w.append(("aZ", S.c("aZ")))
                elif m == 4:
                    w = [("vPAR", vp0 + 1)]
                elif m == 5:
                    w = [("vPAR", vp0 + 2)]
                elif m == 8:
                    w = [("vPAR", vp0 + 3)]
                S.op("pe", par_mm(m), waits=w, inc="pPAR")
                pp = S.c("pPAR")
                if m == 0:
                    S.op("dve", lambda e: e.tensor_copy(dtb[:], ps8[:, 4, :]),
                         waits=[("pPAR", pp)], inc="vPAR")
                elif m == 3:
                    S.op("dve", lambda e: e.tensor_copy(bmat[:, 0:3, :],
                                                        ps8[:, 5:8, :]),
                         waits=[("pPAR", pp)], inc="vPAR")
                elif m == 4:
                    S.op("dve", lambda e: e.tensor_copy(bmat[:, 3, :],
                                                        ps8[:, 4, :]),
                         waits=[("pPAR", pp)], inc="vPAR")
                elif m == 7:
                    S.op("dve", lambda e: e.tensor_copy(cmat[:, 0:3, :],
                                                        ps8[:, 5:8, :]),
                         waits=[("pPAR", pp)], inc="vPAR")
                elif m == 8:
                    S.op("dve", lambda e: e.tensor_copy(cmat[:, 3, :],
                                                        ps8[:, 4, :]),
                         waits=[("pPAR", pp)], inc="vPAR")

        def emit_dt(g):
            vpar = S.c("vPAR")
            for c in range(NCT):
                w = []
                if c == 0:
                    w = [("vPAR", vpar), ("dWD", 16)]
                S.op("pe", (lambda bk, cc: (lambda e: e.matmul(
                    ps8[:, bk, :], wdt[:, cc * 128:(cc + 1) * 128], dtb[:],
                    start=True, stop=True)))(4 + c, c),
                    waits=w, inc="pDT")
            pd = S.c("pDT")

            def exp_fn(e):
                ii = None
                for c in range(NCT):
                    ii = e.activation(delta[:, c, :], ps8[:, 4 + c, :], AF.Exp,
                                      bias=bd[:, c:c + 1], scale=1.0)
                return ii
            S.op("act", exp_fn, waits=[("pDT", pd), ("vB", S.c("vB"))],
                 relax=4)

            def ln_fn(e):
                ii = None
                for c in range(NCT):
                    ii = e.activation(delta[:, c, :], delta[:, c, :], AF.Ln,
                                      bias=consts[:, 2:3], scale=1.0)
                return ii
            S.op("act", ln_fn, inc="aDELTA", relax=4)

        def emit_scan(g):
            adel = S.c("aDELTA")
            state = state_own if own(g) else state_aux

            def ab_wide(e):
                e.tensor_tensor(bmat[:], delta[:], bmat[:], ALU.mult)
                return e.tensor_tensor(bmat[:], bmat[:], xact[:], ALU.mult)
            S.op("dve", ab_wide, waits=[("aDELTA", adel)], inc="vB")

            def ab_dn(e):
                ii = None
                for c in range(NCT):
                    ii = e.tensor_scalar(delta[:, c, :], delta[:, c, :],
                                         a_neg[:, c:c + 1], None, ALU.mult)
                return ii
            S.op("dve", ab_dn, inc="vB", relax=4)

            def a_fn(e):
                ii = None
                for c in range(NCT):
                    ii = e.activation(a_sb[:, c, :], delta[:, c, :], AF.Exp)
                return ii
            S.op("act", a_fn, waits=[("vB", S.c("vB")), ("vSCAN", S.c("vSCAN"))],
                 inc="aAA", relax=4)

            if g == 4:
                S.op("dve", lambda e: e.tensor_scalar(
                    state_own[:], state_aux[:], fl, None, ALU.mult))

            zero_init = (g == 0)

            def scan_fn(e):
                for c in range(NCT):
                    init = 0.0 if zero_init else state[:, c:c + 1]
                    e.tensor_tensor_scan(h_sb[:, c, :], a_sb[:, c, :],
                                         bmat[:, c, :], init, ALU.mult,
                                         ALU.add)
                ii = None
                for c in range(NCT):
                    ii = e.tensor_copy(state[:, c:c + 1], h_sb[:, c, TC - 1:TC])
                return ii
            S.op("dve", scan_fn, waits=[("aAA", S.c("aAA"))], inc="vSCAN",
                 relax=4)

        def emit_y3(g):
            S.op("dve", lambda e: e.tensor_tensor(cmat[:], cmat[:], h_sb[:],
                                                  ALU.mult),
                 waits=[("vSCAN", S.c("vSCAN")), ("aZ", S.c("aZ")),
                        ("pWOUT", S.c("pWOUT"))])

            def y_a(e):
                ii = None
                for c in range(NCT):
                    ii = e.scalar_tensor_tensor(cmat[:, c, :], xact[:, c, :],
                                                dv[:, c:c + 1], cmat[:, c, :],
                                                ALU.mult, ALU.add)
                return ii
            S.op("dve", y_a, relax=4)
            S.op("dve", lambda e: e.tensor_tensor(y3[:], cmat[:], sz[:],
                                                  ALU.mult), inc="vY3")

        def emit_wout_tail(g):
            ci = g - 4
            vy3 = S.c("vY3")
            for pair in range(2):
                for s in (2 * pair, 2 * pair + 1):
                    tt = 4 * ci + s
                    trow = ci * TC + s * 128
                    S.dma("act", r1[:, tt % 2, :], hs_tm[trow:trow + 128, :],
                          waits=[("wXT%d" % (tt % 2),
                                  16 * max(0, tt // 2))], inc="dHSTM%d" % (tt % 2))
                    for dh in range(2):
                        w = [("vY3", vy3)]
                        w.append(("vR1", max(0, S.c("vR1") - 1)))
                        if s == 0 and dh == 0:
                            w += [("vPAR", S.c("vPAR")),
                                  ("aDELTA", S.c("aDELTA")),
                                  ("vGS", S.c("vGS")), ("dWO", 32)]

                        def wo_fn(ss, dd):
                            def f(e):
                                ii = None
                                for n in range(2):
                                    col = 2 * dd + n
                                    for c in range(NCT):
                                        ii = e.matmul(
                                            ps8[:, 4 + 2 * dd + n, :],
                                            y3[:, c, ss * 128:(ss + 1) * 128],
                                            wout[:, c, col * 512:(col + 1) * 512],
                                            start=(c == 0), stop=(c == NCT - 1))
                                return ii
                            return f
                        S.op("pe", wo_fn(s, dh), waits=w, inc="pWOUT")
                        pw = S.c("pWOUT")
                        S.op("dve", (lambda ttt, dd: (lambda e: e.scalar_tensor_tensor(
                            r1[:, ttt % 2, dd * 1024:(dd + 1) * 1024],
                            ps8[:, 4 + 2 * dd:6 + 2 * dd, :].rearrange("p a b -> p (a b)"),
                            1.0, r1[:, ttt % 2, dd * 1024:(dd + 1) * 1024],
                            ALU.mult, ALU.add,
                            accum_out=parts[:, dd:dd + 1])))(tt, dh),
                            waits=[("pWOUT", pw),
                                   ("dHSTM%d" % (tt % 2), 16 * (tt // 2 + 1))],
                            inc="vR1")
                    S.op("dve", (lambda ttt: (lambda e: e.tensor_tensor(
                        sums[:, ttt:ttt + 1], parts[:, 0:1], parts[:, 1:2],
                        ALU.add)))(tt))
                    S.op("act", (lambda ttt: (lambda e: e.activation(
                        junk2k[:], r1[:, ttt % 2, :], AF.Square,
                        accum_out=sumsq[:, ttt:ttt + 1])))(tt),
                        waits=[("vR1", S.c("vR1"))], inc="aSQ")

                t0 = 4 * ci + 2 * pair
                asq = S.c("aSQ")
                S.op("dve", (lambda tz: (lambda e: e.tensor_scalar(
                    negm[:, tz:tz + 2], sums[:, tz:tz + 2], -1.0 / D, None,
                    ALU.mult)))(t0),
                    waits=[("aSQ", asq)])
                S.op("dve", (lambda tz: (lambda e: e.tensor_tensor(
                    lnv[:, 0:2], negm[:, tz:tz + 2], negm[:, tz:tz + 2],
                    ALU.mult)))(t0))
                S.op("dve", (lambda tz: (lambda e: e.scalar_tensor_tensor(
                    lnv[:, 0:2], sumsq[:, tz:tz + 2], 1.0 / D, lnv[:, 0:2],
                    ALU.mult, ALU.subtract)))(t0), inc="vNRM")
                vn = S.c("vNRM")
                S.op("act", lambda e: e.activation(lnv[:, 0:2], lnv[:, 0:2],
                                                   AF.Ln, bias=consts[:, 3:4],
                                                   scale=1.0),
                     waits=[("vNRM", vn)])
                S.op("act", (lambda tz: (lambda e: e.activation(
                    rstd[:, tz:tz + 2], lnv[:, 0:2], AF.Exp, scale=-0.5)))(t0),
                    inc="aRSTD")
                S.op("dve", (lambda tz: (lambda e: e.tensor_scalar(
                    rstdh[:, tz:tz + 2], rstd[:, tz:tz + 2], 0.5, None,
                    ALU.mult)))(t0),
                    waits=[("aRSTD", S.c("aRSTD"))])

                for s in (2 * pair, 2 * pair + 1):
                    tt = 4 * ci + s
                    trow = ci * TC + s * 128
                    S.op("dve", (lambda ttt: (lambda e: e.tensor_scalar(
                        x1bf[:, ttt % 4, :], r1[:, ttt % 2, :],
                        negm[:, ttt:ttt + 1], rstdh[:, ttt:ttt + 1],
                        ALU.add, ALU.mult)))(tt),
                        waits=[("pTP", max(0, tt - 3))], inc="vX1")
                    vx1_of[tt] = S.c("vX1")
                    S.op("dve", (lambda ttt: (lambda e: e.tensor_scalar(
                        r1[:, ttt % 2, :], r1[:, ttt % 2, :],
                        negm[:, ttt:ttt + 1], rstd[:, ttt:ttt + 1],
                        ALU.add, ALU.mult)))(tt), inc="vX1")
                    vx1 = S.c("vX1")
                    S.dma("act", x1tm_sp[trow:trow + 128, :], r1[:, tt % 2, :],
                          waits=[("vX1", vx1)], inc="wXT%d" % (tt % 2))

        def emit_tp(j, vevx_need):
            ci = j - 4
            for s in range(4):
                tt = 4 * ci + s
                trow = ci * TC + s * 128
                w = [("vX1", vx1_of[tt]), ("vTPC", tt)]
                if s == 0:
                    w.append(("vEVX", vevx_need))

                def tp_fn(ttt):
                    def f(e):
                        ii = None
                        for jj in range(NKD):
                            ii = e.transpose(
                                ptp[:, jj // 8, (jj % 8) * 128:(jj % 8 + 1) * 128],
                                x1bf[:, ttt % 4, jj * 128:(jj + 1) * 128], ident[:])
                        return ii
                    return f
                S.op("pe", tp_fn(tt), waits=w, inc="pTP")
                S.op("dve", (lambda ttt: (lambda e: e.tensor_copy(
                    tpst[:], ptp.rearrange("p a b -> p (a b)"))))(tt),
                    waits=[("pTP", S.c("pTP")), ("wXC", 32 * tt)],
                    inc="vTPC")
                vtpc = S.c("vTPC")
                tps3 = tpst.rearrange("p (jj t) -> p jj t", t=128)
                for jh in range(2):
                    S.dma("act",
                          x1cm_sp.rearrange("j p t -> p j t")[
                              :, jh * 8:(jh + 1) * 8, trow:trow + 128],
                          tps3[:, jh * 8:(jh + 1) * 8, :],
                          waits=([("vTPC", vtpc)] if jh == 0 else []),
                          inc="wXC")

        emit_hs_half(0)
        emit_hs_half(1)
        emit_hs_half(2)
        emit_rest_weights()

        gamma_dma(4)

        def chunk_vector_chain(g):
            emit_conv_silu_cast(g)
            if own(g):
                emit_siluz(g)
            emit_params(g)
            emit_dt(g)
            emit_scan(g)
            if own(g):
                emit_y3(g)

        if STAGE >= 1:
            emit_ipx(0)
            emit_hs_half(3)
            emit_ipx(1)
            for g in range(4):
                emit_evacx(g)
                chunk_vector_chain(g)
                gamma_reduce(40)
                gamma_dma(40)
                if g + 2 <= 3:
                    emit_hs_half(2 * (g + 2))
                    emit_hs_half(2 * (g + 2) + 1)
                    emit_ipx(g + 2)
                elif g + 2 == 4:
                    emit_hs_half(8)
                    emit_hs_half(9)
                    emit_ipx(4)

        KSUB = os.environ.get("KSUB", "e")
        DO_WOUT = KSUB >= "b"
        DO_TP = KSUB >= "d"
        DO_FIN = KSUB >= "e"
        if STAGE >= 2:
            for g in range(4, 8):
                emit_evacx(g)
                if g >= 6 and DO_TP:
                    emit_tp(g - 2, 2 * g + 1)
                if g > 4 and DO_WOUT:
                    emit_wout_tail(g - 1)
                emit_ipz(g)
                chunk_vector_chain(g)
                gamma_reduce(24)
                gamma_dma(24)
                if g + 1 <= 7:
                    emit_hs_half(2 * (g + 1))
                    emit_hs_half(2 * (g + 1) + 1)
                    emit_ipx(g + 1)
            if DO_TP:
                emit_tp(6, 15)
            if DO_WOUT:
                emit_wout_tail(7)
            if DO_TP:
                emit_tp(7, 15)
            gamma_dma(256)
            gamma_reduce(256)
            if DO_FIN:
                gamma_final(0)
                gamma_final(1)
        else:
            gamma_dma(256)
            gamma_reduce(256)
            gamma_final(0)
            gamma_final(1)

        BAR = [("pGM", S.c("pGM")), ("vGS", S.c("vGS")),
               ("aRSTD", S.c("aRSTD")), ("pTP", S.c("pTP")), ("vTPC", S.c("vTPC")),
               ("wXT0", S.c("wXT0")), ("wXT1", S.c("wXT1")), ("wXC", S.c("wXC")),
               ("vR1", S.c("vR1")), ("vX1", S.c("vX1")),
               ("vY3", S.c("vY3")), ("vSCAN", S.c("vSCAN")),
               ("pWOUT", S.c("pWOUT")), ("pPAR", S.c("pPAR")),
               ("pDT", S.c("pDT")), ("pIPx", S.c("pIPx")),
               ("pIPz", S.c("pIPz")), ("aZ", S.c("aZ")),
               ("aXA", S.c("aXA")), ("vXB", S.c("vXB")),
               ("aSQ", S.c("aSQ")), ("gG", S.c("gG")), ("gGa", S.c("gGa")),
               ("dWM", n_wm), ("dWX", 64), ("dWZ", 32), ("dWP", 32),
               ("dWO", 32), ("dWD", 16), ("init", 1)]

        w12src = {"w1": w1T, "w2": w2T}
        qsdst = {"w1": q1s, "w2": q2s}
        gsi = {"w1": 0, "w2": 1}
        wQ_A = {}
        fst = {"x1tmld": 0, "r2ev": 0}
        asq_of = {}

        def ffn_half(h, sub=2):
            for j in range(NKD):
                w = []
                if j == 0:
                    w = list(BAR) if h == 0 else [("pA", 88)]
                S.dma("sp", x1cm[:, j, :], x1cm_sp[j, :, h * TH:(h + 1) * TH],
                      waits=w, inc="dX1CM")
            dx1 = S.c("dX1CM")
            if sub < 1:
                return

            gidx = [0]
            vq_of = {}
            dw_of = {}
            sig_of = {}
            hid_of = {}
            pa_of = {}

            def a_quant(win):
                for wname in ("w1", "w2"):
                    for g in range(2):
                        qdst = qb[wname][win % 2][:, g * 8:(g + 1) * 8, :]
                        if h == 0:
                            src = w12src[wname].rearrange("(j p) f -> p j f", p=128)
                            stg = wstage[gidx[0] % 4]
                            gidx[0] += 1
                            wst = [("aSGN", max(0, S.c("aSGN") - 3))]
                            if win == 0 and wname == "w1" and g == 0:
                                wst = wst + BAR
                            slot = "dW12_%s%d%d" % (wname[1], g, win % 2)
                            S.dma("sp", stg[:],
                                  src[:, g * 8:(g + 1) * 8,
                                      win * FWIN:(win + 1) * FWIN],
                                  waits=wst, inc=slot)
                            qwar = [(slot, S.c(slot))]
                            if win >= 2:
                                qwar += [("pA", 2 * (win - 1)),
                                         ("wQ12_%d" % (win % 2), 64 * (win // 2))]
                            wi = gsi[wname]
                            if wname == "w1":
                                sc = gscale[:, wi:wi + 1]
                                S.op("act", (lambda st, qd, s2: (lambda e: e.activation(
                                    qd, st[:], AF.Sign, bias=consts[:, 0:1],
                                    scale=s2)))(stg, qdst, sc),
                                    waits=qwar)
                                S.op("act", (lambda st, s2: (lambda e: e.activation(
                                    sgnbA[:], st[:], AF.Sign, bias=consts[:, 1:2],
                                    scale=s2)))(stg, sc),
                                    waits=[("vQADD", S.c("vQADD"))], inc="aSGN")
                                S.op("dve", (lambda qd: (lambda e: e.tensor_tensor(
                                    qd, qd, sgnbA[:], ALU.add)))(qdst),
                                    waits=[("aSGN", S.c("aSGN"))], inc="vQADD")
                            else:
                                def cmp_fn(st, w2):
                                    def f(e):
                                        e.tensor_scalar(
                                            tmpq[:, 0, :, :], st,
                                            gscale[:, 3 + w2:4 + w2],
                                            2.0, ALU.is_gt, ALU.mult)
                                        return e.tensor_scalar(
                                            tmpq[:, 1, :, :], st,
                                            gscale[:, 6 + w2:7 + w2],
                                            2.0, ALU.is_lt, ALU.mult)
                                    return f
                                S.op("dve", cmp_fn(stg[:], wi), waits=qwar,
                                     inc="aSGN", relax=2)
                                S.op("dve", (lambda qd: (lambda e: e.tensor_tensor(
                                    qd, tmpq[:, 0, :, :], tmpq[:, 1, :, :],
                                    ALU.subtract)))(qdst),
                                    inc="vQADD")
                            vq = S.c("vQADD")
                            S.dma("pool",
                                  qsdst[wname].rearrange("(j p) f -> p j f", p=128)[
                                      :, g * 8:(g + 1) * 8,
                                      win * FWIN:(win + 1) * FWIN],
                                  qdst, waits=[("vQADD", vq)],
                                  inc="wQ12_%d" % (win % 2))
                        else:
                            w = [("pA", 86 + 2 * win)]
                            if win == 0 and wname == "w1" and g == 0:
                                w.append(("wQ12_0", wQ_A["t0"]))
                                w.append(("wQ12_1", wQ_A["t1"]))
                            slot = "dW12_%s%d%d" % (wname[1], g, win % 2)
                            S.dma("sp", qdst,
                                  qsdst[wname].rearrange("(j p) f -> p j f", p=128)[
                                      :, g * 8:(g + 1) * 8,
                                      win * FWIN:(win + 1) * FWIN],
                                  waits=w, inc=slot)
                vq_of[win] = S.c("vQADD")
                dw_of[win] = [("dW12_%s%d%d" % (wn, gg, win % 2),
                               S.c("dW12_%s%d%d" % (wn, gg, win % 2)))
                              for wn in ("1", "2") for gg in range(2)]
                if h == 0 and win % 2 == 1:
                    gamma3_batch(4, g3slots)

            def a_mm_step(fc):
                par = fc % 2
                if h == 0:
                    wmm = [("vQADD", vq_of[fc]), ("dX1CM", dx1)]
                else:
                    wmm = [("dX1CM", dx1)] + dw_of[fc]
                if fc >= 2:
                    wmm += [("aSIG", sig_of[fc - 2]), ("vHID", hid_of[fc - 2])]
                elif h == 1:
                    wmm += [("aSIG", wQ_A["sig"][fc]), ("vHID", wQ_A["hid"][fc]),
                            ("vR2S", wQ_A["r2ev"])]

                def a_mm(wname_i, bank0, fci):
                    qsb = qb[wname_i][fci % 2]
                    def f(e):
                        ii = None
                        for n in range(2):
                            for k in range(NKD):
                                ii = e.matmul(ps8[:, bank0 + n, :],
                                              qsb[:, k, 0:128],
                                              x1cm[:, k, n * 512:(n + 1) * 512],
                                              start=(k == 0),
                                              stop=(k == NKD - 1))
                        return ii
                    return f
                b0 = 0 if par == 0 else 4
                S.op("pe", a_mm("w1", b0, fc), waits=wmm, inc="pA")
                S.op("pe", a_mm("w2", b0 + 2, fc), inc="pA")
                pa_of[fc] = S.c("pA")

            def a_evac(fc):
                b0 = 0 if fc % 2 == 0 else 4
                S.op("act", (lambda bk, ib: (lambda e: e.activation(
                    sigb[ib].rearrange("p (a b) -> p a b", a=2),
                    ps8[:, bk:bk + 2, :], AF.Sigmoid, scale=1.0)))(b0, fc % 2),
                    waits=[("pA", pa_of[fc])], inc="aSIG")
                sig_of[fc] = S.c("aSIG")
                hwait = [("aSIG", sig_of[fc])]
                if h == 1 and fc == 0:
                    hwait.append(("pBf", 88))
                S.op("dve", (lambda bk, fci, ib: (lambda e: e.scalar_tensor_tensor(
                    hidden[:, fci, :].rearrange("p (a b) -> p a b", a=2),
                    ps8[:, bk + 2:bk + 4, :], 0.5,
                    sigb[ib].rearrange("p (a b) -> p a b", a=2),
                    ALU.mult, ALU.mult)))(b0, fc, fc % 2),
                    waits=hwait, inc="vHID")
                hid_of[fc] = S.c("vHID")

            for win in range(NWIN + 2):
                if win < NWIN:
                    a_quant(win)
                if win >= 1 and win - 1 < NWIN:
                    a_mm_step(win - 1)
                if win >= 2:
                    a_evac(win - 2)
            if h == 0:
                wQ_A["t0"] = S.c("wQ12_0")
                wQ_A["t1"] = S.c("wQ12_1")
                wQ_A["sig"] = (sig_of[NWIN - 2], sig_of[NWIN - 1])
                wQ_A["hid"] = (hid_of[NWIN - 2], hid_of[NWIN - 1])
                gamma3_batch(88, g3slots)
                gamma_final(2, extra=[("vHID", S.c("vHID")),
                                      ("aSIG", S.c("aSIG"))])

            if sub < 2:
                return

            vhid_all = S.c("vHID")
            for dg in range(4):
                for grp in range(22):
                    if h == 0:
                        src = w3T.rearrange("(q p) d -> p q d", p=128)
                        stg = w3stage[grp % 2]
                        slot3 = "dW3s%d" % (grp % 2)
                        S.dma("sp", stg[:],
                              src[:, grp * 2:(grp + 1) * 2,
                                  dg * 512:(dg + 1) * 512],
                              waits=[("aSGN", max(0, S.c("aSGN") - 1))], inc=slot3)
                        sc = gscale[:, 2:3]
                        qdst = q3b[grp % 3]
                        sgn_w = [(slot3, S.c(slot3)),
                                 ("pBf", max(0, S.c("pBf") - 2)),
                                 ("wQ3_%d" % (grp % 3),
                                  S.c("wQ3_%d" % (grp % 3)))]
                        if dg == 0 and grp == 0:
                            sgn_w.append(("vGS", S.c("vGS")))
                        tq2 = tmpq.rearrange("p a j b -> p a (j b)").rearrange(
                            "p a (q b) -> p a q b", q=2)
                        S.op("dve", (lambda st: (lambda e: e.tensor_scalar(
                            tq2[:, 0, :, :], st[:], gscale[:, 5:6],
                            2.0, ALU.is_gt, ALU.mult)))(stg),
                            waits=sgn_w)
                        S.op("dve", (lambda st: (lambda e: e.tensor_scalar(
                            tq2[:, 1, :, :], st[:], gscale[:, 8:9],
                            2.0, ALU.is_lt, ALU.mult)))(stg),
                            inc="aSGN")
                        S.op("dve", (lambda qd: (lambda e: e.tensor_tensor(
                            qd[:], tq2[:, 0, :, :], tq2[:, 1, :, :],
                            ALU.subtract)))(qdst),
                            inc="vQADD")
                        vq = S.c("vQADD")
                        S.dma("pool",
                              q3s.rearrange("(q p) d -> p q d", p=128)[
                                  :, grp * 2:(grp + 1) * 2,
                                  dg * 512:(dg + 1) * 512],
                              qdst[:], waits=[("vQADD", vq)], inc="wQ3_%d" % (grp % 3))
                    else:
                        wq3 = [("pBf", max(0, S.c("pBf") - 2))]
                        if dg == 0 and grp == 0:
                            for qsem in ("wQ3_0", "wQ3_1", "wQ3_2"):
                                wq3.append((qsem, S.c(qsem)))
                        slotq = "dW3q%d" % (grp % 3)
                        S.dma("sp", q3b[grp % 3][:],
                              q3s.rearrange("(q p) d -> p q d", p=128)[
                                  :, grp * 2:(grp + 1) * 2,
                                  dg * 512:(dg + 1) * 512],
                              waits=wq3, inc=slotq)
                    if h == 0:
                        wmm = [("vQADD", S.c("vQADD"))]
                    else:
                        wmm = [(slotq, S.c(slotq))]
                    if grp == 0 and dg == 0:
                        wmm.append(("vHID", vhid_all))
                    if grp == 0 and dg > 0:
                        wmm.append(("vR2S", S.c("vR2S")))

                    def b_mm(grpi):
                        def f(e):
                            ii = None
                            for q in range(2):
                                ft = grpi * 2 + q
                                for s in range(8):
                                    ii = e.matmul(ps8[:, s, :],
                                                  hidden[:, ft, s * 128:(s + 1) * 128],
                                                  q3b[grpi % 3][:, q, :],
                                                  start=(ft == 0),
                                                  stop=(ft == NFC - 1))
                            return ii
                        return f
                    S.op("pe", b_mm(grp), waits=wmm, inc="pBf")
                pb = S.c("pBf")

                for s in range(8):
                    tt = h * 8 + s
                    trow = h * TH + s * 128
                    li = fst["x1tmld"]
                    S.dma("sp", x1tmld[li % 2][:],
                          x1tm_sp[trow:trow + 128, dg * 512:(dg + 1) * 512],
                          waits=[("wXT0", 128), ("wXT1", 128),
                                 ("vR2S", max(0, li - 1))],
                          inc="dXT%d" % (li % 2))
                    fst["x1tmld"] += 1
                    ri = fst["r2ev"]
                    fst["r2ev"] += 1
                    evw = [("pBf", pb),
                           ("dXT%d" % (li % 2), S.c("dXT%d" % (li % 2))),
                           ("wR2_%d" % (ri % 2), max(0, 16 * (ri // 2)))]
                    if ri >= 2:
                        evw.append(("aSQ", asq_of[ri - 2]))
                    S.op("dve", (lambda ss, lii, rii: (lambda e: e.scalar_tensor_tensor(
                        r2t[rii % 2][:], ps8[:, ss, :], 1.0, x1tmld[lii % 2][:],
                        ALU.mult, ALU.add, accum_out=partsB[:, ss:ss + 1])))(s, li, ri),
                        waits=evw, inc="vR2S")
                    ev = S.c("vR2S")
                    S.dma("pool", r2_sp[trow:trow + 128, dg * 512:(dg + 1) * 512],
                          r2t[ri % 2][:], waits=[("vR2S", ev)],
                          inc="wR2_%d" % (ri % 2))
                    S.op("act", (lambda rii, ss: (lambda e: e.activation(
                        junk512[:], r2t[rii % 2][:], AF.Square,
                        accum_out=partsB[:, 8 + ss:9 + ss])))(ri, s),
                        waits=[("vR2S", ev)], inc="aSQ")
                    asq_of[ri] = S.c("aSQ")

                t0c = h * 8
                fw = [("aSQ", asq_of[fst["r2ev"] - 1])]
                if dg == 0:
                    S.op("dve", (lambda tz: (lambda e: e.tensor_copy(
                        sums[:, tz:tz + 8], partsB[:, 0:8])))(t0c), waits=fw)
                    S.op("dve", (lambda tz: (lambda e: e.tensor_copy(
                        sumsq[:, tz:tz + 8], partsB[:, 8:16])))(t0c),
                        inc="vR2EV")
                else:
                    S.op("dve", (lambda tz: (lambda e: e.tensor_tensor(
                        sums[:, tz:tz + 8], sums[:, tz:tz + 8],
                        partsB[:, 0:8], ALU.add)))(t0c), waits=fw)
                    S.op("dve", (lambda tz: (lambda e: e.tensor_tensor(
                        sumsq[:, tz:tz + 8], sumsq[:, tz:tz + 8],
                        partsB[:, 8:16], ALU.add)))(t0c),
                        inc="vR2EV")

                if dg == 3:
                    vre = S.c("vR2EV")
                    S.op("dve", (lambda tz: (lambda e: e.tensor_scalar(
                        negm[:, tz:tz + 8], sums[:, tz:tz + 8], -1.0 / D,
                        None, ALU.mult)))(t0c),
                        waits=[("vR2EV", vre), ("aRSTD", S.c("aRSTD"))])
                    S.op("dve", (lambda tz: (lambda e: e.tensor_tensor(
                        lnv[:], negm[:, tz:tz + 8], negm[:, tz:tz + 8],
                        ALU.mult)))(t0c))
                    S.op("dve", (lambda tz: (lambda e: e.scalar_tensor_tensor(
                        lnv[:], sumsq[:, tz:tz + 8], 1.0 / D, lnv[:],
                        ALU.mult, ALU.subtract)))(t0c), inc="vNRM")
                    vn = S.c("vNRM")
                    S.op("act", lambda e: e.activation(
                        lnv[:], lnv[:], AF.Ln, bias=consts[:, 3:4], scale=1.0),
                        waits=[("vNRM", vn)])
                    S.op("act", (lambda tz: (lambda e: e.activation(
                        rstd[:, tz:tz + 8], lnv[:], AF.Exp, scale=-0.5)))(t0c),
                        inc="aRSTD")
                    ars = S.c("aRSTD")
                    for s in range(8):
                        tt = h * 8 + s
                        trow = h * TH + s * 128
                        S.dma("pool", r2full[0][:], r2_sp[trow:trow + 128, :],
                              waits=[("wR2_0", S.c("wR2_0")),
                                     ("wR2_1", S.c("wR2_1")),
                                     ("gG3", 88),
                                     ("vNRM", max(0, S.c("vNRM") - 1))],
                              inc="dR2RD")
                        drr = S.c("dR2RD")
                        S.op("dve", (lambda ttt: (lambda e: e.tensor_scalar(
                            outst[0][:], r2full[0][:], negm[:, ttt:ttt + 1],
                            rstd[:, ttt:ttt + 1], ALU.add, ALU.mult)))(tt),
                            waits=[("aRSTD", ars), ("dR2RD", drr),
                                   ("dOUT", max(0, 16 * tt))], inc="vNRM")
                        vn2 = S.c("vNRM")
                        S.dma("pool", out_d[trow:trow + 128, :], outst[0][:],
                              waits=[("vNRM", vn2)], inc="dOUT")

        if STAGE >= 3:
            ffn_half(0, sub=(0 if STAGE == 30 else (1 if STAGE == 31 else 2)))
        wQ_A["r2ev"] = S.c("vR2S")
        if STAGE >= 4:
            ffn_half(1, sub=(0 if STAGE == 41 else (1 if STAGE == 42 else (3 if STAGE == 43 else 2))))
        if STAGE != 4 and STAGE < 99:
            wtail = [(n, S.c(n)) for n in (
                "vGS", "wXT0", "wXT1", "wXC", "pTP", "vTPC", "wQ12_0",
                "wQ12_1", "wQ3_0", "wQ3_1", "wQ3_2", "dHS0", "dHS1", "dHS2",
                "dHSTM0", "dHSTM1", "vSCAN",
                "aSIG", "vHID", "wR2_0", "wR2_1", "dOUT", "gG", "gGa",
                "pIPx", "pIPz", "pWOUT")]
            S.dma("sp", out_d[0:128, :], outst[0][:], waits=wtail, inc="dOUT")

        nout = S.c("dOUT")
        S.op("act", lambda e: e.nop(), waits=[("dOUT", nout)])
        S.op("sp", lambda e: e.nop(), waits=[("dOUT", nout)])
        S.op("dve", lambda e: e.nop(), waits=[("dOUT", nout)])

        self_sems = {"dve": ctx.enter_context(nc.semaphore("selfDVE")),
                     "act": ctx.enter_context(nc.semaphore("selfACT")),
                     "pool": ctx.enter_context(nc.semaphore("selfPOOL"))}
        block = ctx.enter_context(nc.Block())
        S.emit(block, self_sems)

    build_program.last_sched = S
    return nc


_NC_CACHE = {}


def kernel(L0_attn_hidden_states, L0_attn_W_in_x, L0_attn_W_in_z, L0_attn_conv_w,
           L0_attn_conv_b, L0_attn_W_param, L0_attn_W_dt, L0_attn_b_dt,
           L0_attn_A_log, L0_attn_D, L0_attn_W_out, L0_ffn_w1, L0_ffn_w2, L0_ffn_w3):
    f32 = np.float32
    hs = np.ascontiguousarray(np.asarray(L0_attn_hidden_states, dtype=f32))
    c = lambda a: np.ascontiguousarray(np.asarray(a, dtype=f32))

    winxT = c(np.asarray(L0_attn_W_in_x, dtype=f32).T)
    winzT = c(np.asarray(L0_attn_W_in_z, dtype=f32).T)
    wparamT = c(np.asarray(L0_attn_W_param, dtype=f32).T)
    wdtT = c(np.repeat(np.asarray(L0_attn_W_dt, dtype=f32), 16, axis=0).T)
    woutT = c(np.asarray(L0_attn_W_out, dtype=f32).T)
    w1T = c(np.asarray(L0_ffn_w1, dtype=f32).T)
    w2T = c(np.asarray(L0_ffn_w2, dtype=f32).T)
    w3T = c(np.asarray(L0_ffn_w3, dtype=f32).T)
    convw = c(np.asarray(L0_attn_conv_w, dtype=f32).reshape(4, 128, 4)
              .transpose(1, 0, 2).reshape(128, 16))
    convb = c(np.asarray(L0_attn_conv_b, dtype=f32).reshape(4, 128).T)
    bdt = c(np.repeat(np.asarray(L0_attn_b_dt, dtype=f32), 16).reshape(4, 128).T)
    aflat = c(np.asarray(L0_attn_A_log, dtype=f32).reshape(4, 128).T)
    dvec = c(np.asarray(L0_attn_D, dtype=f32).reshape(4, 128).T)

    if "nc" not in _NC_CACHE:
        _NC_CACHE["nc"] = build_program()
    nc = _NC_CACHE["nc"]
    in_maps = []
    for core in range(8):
        b, h = core // 2, core % 2
        hs_tm_c = np.ascontiguousarray(hs[b, h * T:(h + 1) * T, :])
        in_maps.append({
            "hs_tm": hs_tm_c,
            "hs_cm": np.ascontiguousarray(hs_tm_c.T),
            "aux_cm": np.ascontiguousarray(hs[b, 0:T, :].T),
            "winxT": winxT, "winzT": winzT, "wparamT": wparamT, "wdtT": wdtT,
            "woutT": woutT, "w1T": w1T, "w2T": w2T, "w3T": w3T,
            "convw": convw, "convb": convb, "bdt": bdt, "aflat": aflat,
            "dvec": dvec,
            "flag": np.full((128, 1), float(h), dtype=f32),
        })
    res = run_bass_kernel_spmd(nc, in_maps, core_ids=list(range(8)))
    out = np.empty((4, 4096, D), dtype=f32)
    for core in range(8):
        b, h = core // 2, core % 2
        out[b, h * T:(h + 1) * T, :] = np.asarray(res.results[core]["out"])
    return out


if __name__ == "__main__":
    import jax
    _cpu = jax.devices("cpu")[0]
    with jax.default_device(_cpu):
        import reference as _r
        ins = {k: np.asarray(v) for k, v in _r.setup_inputs(0).items()}
    got = kernel(**ins)
    with jax.default_device(_cpu):
        want = np.asarray(_r.reference(**{k: jax.device_put(v, _cpu) for k, v in ins.items()}))
    num = float(np.mean((got - want) ** 2))
    den = float(np.mean(want ** 2))
    print("resid var ratio:", num / den)


# version 27 (final)
# speedup vs baseline: 1.0543x; 1.0202x over previous
import os
import numpy as np
from contextlib import ExitStack

import concourse.bass as bass
import concourse.mybir as mybir
from concourse.bass_utils import run_bass_kernel_spmd

F32 = mybir.dt.float32
BF16 = mybir.dt.bfloat16
AF = mybir.ActivationFunctionType
ALU = mybir.AluOpType
AX = mybir.AxisListType

D = 2048
T = 2048
TC = 512
NCH = T // TC
DI = 512
NCT = DI // 128
NKD = D // 128
NPAR = 9
NPAR_AUX = 5
FH = 5632
NFC = FH // 128
TH = 1024
FWIN = 128
NWIN = FH // FWIN
EPS_LN = 1e-5
EPS_Q = 1e-8
NW_ELEM = float(FH * D)

ENGINES = ("pe", "dve", "act", "pool", "sp")


class Sched:
    def __init__(self, nc):
        self.nc = nc
        self.streams = {e: [] for e in ENGINES}
        self.counts = {}
        self.handles = {}

    def sem(self, ctx, name):
        h = ctx.enter_context(self.nc.semaphore(name))
        self.counts[name] = 0
        self.handles[name] = h

    def c(self, name):
        return self.counts[name]

    def op(self, eng, fn, waits=(), inc=None, amt=1, relax=1):
        w = [(self.handles[s], v) for (s, v) in waits if v > 0]
        if not hasattr(self, "debug"):
            self.debug = []
        self.debug.append((eng, [(s, v) for (s, v) in waits if v > 0], inc, amt))
        if inc is not None:
            self.counts[inc] += amt
            self.streams[eng].append((w, fn, self.handles[inc], amt, relax))
        else:
            self.streams[eng].append((w, fn, None, 0, relax))

    def dma(self, queue, out, in_, waits=(), inc=None):
        self.op(queue, (lambda o, i: (lambda e: e.dma_start(o, i)))(out, in_),
                waits=waits, inc=inc, amt=16)

    def dmat(self, queue, out, in_, waits=(), inc=None):
        self.op(queue,
                (lambda o, i: (lambda e: e.dma_start_transpose(o, i)))(out, in_),
                waits=waits, inc=inc, amt=16)

    def emit(self, block, self_sems):
        nc = self.nc
        eng_obj = {"pe": nc.tensor, "dve": nc.vector, "act": nc.scalar,
                   "pool": nc.gpsimd, "sp": nc.sync}

        class SerialProxy:
            _SKIP = {"wait_ge", "dma_start", "dma_start_transpose", "nop",
                     "engine_nop", "sem_inc", "sem_clear"}

            def __init__(self, eng, sem):
                self._eng = eng
                self._sem = sem
                self._n = 0
                self._relax = 1
                self._base = 0

            def __getattr__(self, name):
                real = getattr(self._eng, name)
                if name in self._SKIP or not callable(real):
                    return real

                def wrap(*a, **kw):
                    tgt = max(self._base, self._n - (self._relax - 1))
                    if tgt > 0:
                        self._eng.wait_ge(self._sem, tgt)
                    instr = real(*a, **kw)
                    if instr is not None:
                        instr.then_inc(self._sem, 1)
                        self._n += 1
                    return instr
                return wrap

        def mk(ename):
            def body(_):
                raw = eng_obj[ename]
                e = raw
                prox = None
                if ename in self_sems:
                    prox = SerialProxy(raw, self_sems[ename])
                    e = prox
                for (waits, fn, inc_h, amt, relax) in self.streams[ename]:
                    for (h, v) in waits:
                        raw.wait_ge(h, v)
                    n0 = prox._n if prox else 0
                    if prox is not None:
                        prox._relax = relax
                        prox._base = prox._n
                    instr = fn(e)
                    if prox is not None:
                        prox._relax = 1
                    if inc_h is not None:
                        assert instr is not None, f"{ename} op with inc returned None"
                        if prox is not None and prox._n > n0:
                            raw.wait_ge(self_sems[ename], prox._n)
                            raw.sem_inc(inc_h, amt)
                        else:
                            instr.then_inc(inc_h, amt)
            return body

        block.tensor(mk("pe"))
        block.vector(mk("dve"))
        block.scalar(mk("act"))
        block.gpsimd(mk("pool"))
        block.sync(mk("sp"))


def build_program():
    STAGE = int(os.environ.get("KSTAGE", "4"))
    nc = bass.Bass("TRN2", target_bir_lowering=False)

    def din(name, shape, dtype=F32):
        return nc.dram_tensor(name, shape, dtype, kind="ExternalInput").ap()

    hs_tm = din("hs_tm", [T, D])
    hs_cm = din("hs_cm", [D, T])
    aux_cm = din("aux_cm", [D, T])
    winxT = din("winxT", [D, DI])
    winzT = din("winzT", [D, DI])
    wparamT = din("wparamT", [DI, 1152])
    wdtT = din("wdtT", [128, DI])
    woutT = din("woutT", [DI, D])
    w1T = din("w1T", [D, FH])
    w2T = din("w2T", [D, FH])
    w3T = din("w3T", [FH, D])
    convw = din("convw", [128, 16])
    convb = din("convb", [128, NCT])
    bdt = din("bdt", [128, NCT])
    aflat = din("aflat", [128, NCT])
    dvecd = din("dvec", [128, NCT])
    flag_d = din("flag", [128, 1])

    out_d = nc.dram_tensor("out", [T, D], F32, kind="ExternalOutput").ap()

    _dbg = "ExternalOutput" if os.environ.get("KDBG") else "Internal"
    x1tm_sp = nc.dram_tensor("x1tm_sp", [T, D], F32, kind=_dbg).ap()
    x1cm_sp = nc.dram_tensor("x1cm_sp", [NKD, 128, T], BF16, kind=_dbg).ap()
    r2_sp = nc.dram_tensor("r2_sp", [T, D], F32, kind=_dbg).ap()
    q1s = nc.dram_tensor("q1s", [D, FH], BF16, kind=_dbg).ap()
    q2s = nc.dram_tensor("q2s", [D, FH], BF16, kind=_dbg).ap()
    q3s = nc.dram_tensor("q3s", [FH, D], BF16, kind=_dbg).ap()

    S = Sched(nc)
    ctx = ExitStack()
    with ctx:
        sb = lambda name, shape, dt=F32: ctx.enter_context(nc.sbuf_tensor(name, shape, dt))

        consts = sb("consts", [128, 4])
        ones_col = sb("ones_col", [128, 1])
        ones_row = sb("ones_row", [1, 128])
        gtmp = sb("gtmp", [1, 1])
        gscale = sb("gscale", [128, 9])
        smallw = sb("smallw", [128, 33])
        cw = smallw[:, 0:16]
        cb = smallw[:, 16:20]
        bd = smallw[:, 20:24]
        afl = smallw[:, 24:28]
        dv = smallw[:, 28:32]
        fl = smallw[:, 32:33]
        a_neg = sb("a_neg", [128, NCT])
        state_aux = sb("state_aux", [128, NCT])
        state_own = sb("state_own", [128, NCT])
        sums = sb("sums", [128, 16])
        sumsq = sb("sumsq", [128, 16])
        negm = sb("negm", [128, 16])
        rstd = sb("rstd", [128, 16])
        rstdh = sb("rstdh", [128, 16])
        lnt = sb("lnt", [128, 4])
        lnv = sb("lnv", [128, 8])
        parts = sb("parts", [128, 2])
        gcols = sb("gcols", [128, 344])

        ident = sb("ident", [128, 128], BF16)
        ps8 = ctx.enter_context(nc.psum_tensor("ps8", [128, 8, 512], F32))
        ptp = ps8[:, 0:2, :].bitcast(BF16)

        mctx = ExitStack()
        msb = lambda name, shape, dt=F32: mctx.enter_context(nc.sbuf_tensor(name, shape, dt))
        winx = msb("winx", [128, NKD, DI], BF16)
        winz = msb("winz", [128, NKD, DI], BF16)
        wpar = msb("wpar", [128, NCT, 1152], BF16)
        wdt = msb("wdt", [128, DI], BF16)
        wout = msb("wout", [128, NCT, D], BF16)
        hsb = msb("hsb", [128, 3, 8, TC], BF16)
        x_sb = msb("x_sb", [128, NCT, TC + 3])
        xtail = msb("xtail", [128, NCT, 3])
        xact = msb("xact", [128, NCT, TC])
        xactb = msb("xactb", [128, NCT, TC], BF16)
        sz = msb("sz", [128, NCT, TC])
        bmat = msb("bmat", [128, NCT, TC])
        cmat = msb("cmat", [128, NCT, TC])
        dtb = msb("dtb", [128, TC], BF16)
        delta = msb("delta", [128, NCT, TC])
        a_sb = msb("a_sb", [128, NCT, TC])
        h_sb = msb("h_sb", [128, NCT, TC])
        y3 = msb("y3", [128, NCT, TC], BF16)
        r1 = msb("r1", [128, 2, D])
        x1bf = msb("x1bf", [128, 4, D], BF16)
        tpst = msb("tpst", [128, D], BF16)
        junk2k = msb("junk2k", [128, D], BF16)
        gstage = [msb(f"gstage_{i}", [128, 704]) for i in range(3)]
        mctx.close()

        x1cm = sb("x1cm", [128, NKD, TH], BF16)
        hidden = sb("hidden", [128, NFC, TH], BF16)
        qb = {w: [sb(f"qb_{w}_{i}", [128, 16, FWIN], BF16) for i in range(2)]
              for w in ("w1", "w2")}
        sgnb = sb("sgnb", [128, 1024], BF16)
        sgnbA = sgnb.rearrange("p (a b) -> p a b", a=8)
        wstage = [sb(f"wstage_{i}", [128, 8, FWIN]) for i in range(4)]
        tmpq = sb("tmpq", [128, 2, 8, FWIN], BF16)
        sigb = [sb(f"sigb_{i}", [128, TH], BF16) for i in range(2)]
        q3b = [sb(f"q3b_{i}", [128, 2, 512], BF16) for i in range(3)]
        w3stage = [sb(f"w3stage_{i}", [128, 2, 512]) for i in range(2)]
        x1tmld = [sb(f"x1tmld_{i}", [128, 512]) for i in range(2)]

        r2t = [sb(f"r2t_{i}", [128, 512]) for i in range(2)]
        partsB = sb("partsB", [128, 16])
        junk512 = sb("junk512", [128, 512])
        r2full = [sb("r2full_0", [128, D])]
        outst = [sb("outst_0", [128, D])]
        g3slots = [r2full[0][:, 0:1024], r2full[0][:, 1024:2048],
                   outst[0][:, 0:1024], outst[0][:, 1024:2048]]

        for name in ("dHS0", "dHS1", "dHS2", "dHSTM0", "dHSTM1", "dWM",
                     "dWX", "dWZ", "dWP", "dWO", "dWD",
                     "dG0", "dG1", "dG3", "dX1CM",
                     "dW12_100", "dW12_101", "dW12_110", "dW12_111",
                     "dW12_200", "dW12_201", "dW12_210", "dW12_211",
                     "dW3s0", "dW3s1", "dW3s2", "dW3q0", "dW3q1", "dW3q2",
                     "dXT0", "dXT1",
                     "dR2RD", "dOUT", "wXT0", "wXT1", "wXC", "pTP", "vTPC",
                     "wQ12_0", "wQ12_1", "wQ3_0", "wQ3_1", "wQ3_2",
                     "wR2_0", "wR2_1",
                     "pIPx", "pIPz", "pPAR", "pDT", "pWOUT", "pGM",
                     "pA", "pB", "pBf",
                     "gG", "gGa", "gG3", "dG30", "dG31", "dG32", "dG33", "dG2", "vGS",
                     "vEVX", "vXB", "vPAR", "vB", "vSCAN", "vY3", "vR1", "vX1",
                     "vQADD", "vHID", "vR2S", "vR2EV", "vNRM",
                     "aZ", "aXA", "aDELTA", "aAA", "aSQ", "aRSTD", "aSGN", "aSIG",
                     "init"):
            S.sem(ctx, name)

        def init_fn(e):
            e.memset(ident[:], 0.0)
            e.affine_select(out=ident[:], in_=ident[:], compare_op=ALU.not_equal,
                            fill=1.0, base=0, pattern=[[-1, 128]], channel_multiplier=1)
            e.memset(consts[:, 0:1], -0.5)
            e.memset(consts[:, 1:2], 0.5)
            e.memset(consts[:, 2:3], 1.0)
            e.memset(consts[:, 3:4], EPS_LN)
            e.memset(ones_col[:], 1.0)
            return e.memset(ones_row[:], 1.0)
        S.op("pool", init_fn, inc="init")

        S.dma("sp", smallw[:, 0:16], convw[:], inc="dWM")
        S.dma("sp", cb, convb[:], inc="dWM")
        S.dma("sp", bd, bdt[:], inc="dWM")
        S.dma("sp", afl, aflat[:], inc="dWM")
        S.dma("sp", dv, dvecd[:], inc="dWM")
        S.dma("sp", fl, flag_d[:], inc="dWM")
        n_wm = S.c("dWM")

        winx_r = winxT.rearrange("(j p) f -> p j f", p=128)
        for j0 in range(0, NKD, 4):
            S.dma("pool", winx[:, j0:j0 + 4, :], winx_r[:, j0:j0 + 4, :],
                  waits=[("init", 1)], inc="dWX")
        n_winx = S.c("dWX")

        def emit_hs_half(h):
            g = h // 2
            j = h % 2
            sl = h % 3
            src = hs_cm if g >= 4 else aux_cm
            ci = g % 4
            src_r = src.rearrange("(k p) t -> p k t", p=128)
            w = []
            hprev = h - 3
            if hprev >= 0:
                gp = hprev // 2
                if gp <= 3:
                    w.append(("pIPx", gp + 1))
                else:
                    w.append(("pIPz", gp - 3))
            for q in range(2):
                k0 = 8 * j + 4 * q
                S.dma("pool", hsb[:, sl, 4 * q:4 * q + 4, :],
                      src_r[:, k0:k0 + 4, ci * TC:(ci + 1) * TC],
                      waits=(w if q == 0 else []), inc="dHS%d" % sl)

        gma = {"dma": 0, "red": 0, "dma3": 0, "red3": 0}
        w3r = w3T.rearrange("(j p) d -> p j d", p=128)
        gtiles = []
        for wi, src in ((0, w1T), (1, w2T)):
            for blk in range(16):
                for q in range(8):
                    gtiles.append((src[blk * 128:(blk + 1) * 128,
                                       q * 704:(q + 1) * 704], 704))
        NGT = len(gtiles)

        def g_eng(gi):
            return "dve" if (gi // 8) % 2 == 0 else "act"

        def g_cnt(gi):
            full, rem = (gi + 1) // 16, (gi + 1) % 16
            if g_eng(gi) == "dve":
                return full * 8 + min(rem, 8)
            return full * 8 + max(0, rem - 8)

        def g_wait(gi):
            return (("gG" if g_eng(gi) == "dve" else "gGa"), g_cnt(gi))

        g3tiles = []
        for j in range(44):
            for k in range(2):
                g3tiles.append((w3r[:, j, k * 1024:(k + 1) * 1024], 1024))

        def gamma_dma(n):
            for _ in range(n):
                gi = gma["dma"]
                if gi >= NGT:
                    return
                tile, width = gtiles[gi]
                wv = [g_wait(gi - 3)] if gi >= 3 else []
                S.dma("sp", gstage[gi % 3][:, 0:width], tile,
                      waits=wv, inc="dG%d" % (gi % 3))
                gma["dma"] += 1

        def gamma_reduce(n, eng="dve"):
            for _ in range(n):
                gi = gma["red"]
                if gi >= NGT:
                    return
                width = gtiles[gi][1]
                if g_eng(gi) == "dve":
                    S.op("dve",
                         (lambda g, cc, wd: (lambda e: e.tensor_reduce(
                             gcols[:, cc:cc + 1], gstage[g % 3][:, 0:wd], AX.X,
                             ALU.add, apply_absolute_value=True)))(gi, gi, width),
                         waits=[("dG%d" % (gi % 3), 16 * (gi // 3 + 1))], inc="gG")
                else:
                    S.op("act",
                         (lambda g, cc, wd: (lambda e: e.activation(
                             junk2k[:, 0:wd], gstage[g % 3][:, 0:wd], AF.Abs,
                             accum_out=gcols[:, cc:cc + 1])))(gi, gi, width),
                         waits=[("dG%d" % (gi % 3), 16 * (gi // 3 + 1))], inc="gGa")
                gma["red"] += 1

        def gamma3_batch(n, slots):
            for _ in range(n):
                gi = gma["dma3"]
                if gi >= 88:
                    break
                tile, width = g3tiles[gi]
                S.dma("sp", slots[gi % 4][:, 0:width],
                      tile, waits=[("gG3", max(0, gi - 3))],
                      inc="dG3%d" % (gi % 4))
                gma["dma3"] += 1
            batch = []
            for _ in range(n):
                gi = gma["red3"]
                if gi >= 88:
                    break
                batch.append(gi)
                gma["red3"] += 1
            if batch:
                bw = [("dG3%d" % (gi % 4), 16 * (gi // 4 + 1)) for gi in batch]

                def red_fn(tiles):
                    def f(e):
                        ii = None
                        for gi in tiles:
                            wd = g3tiles[gi][1]
                            ii = e.tensor_reduce(
                                gcols[:, 256 + gi:257 + gi],
                                slots[gi % 4][:, 0:wd], AX.X,
                                ALU.add, apply_absolute_value=True)
                        return ii
                    return f
                S.op("dve", red_fn(list(batch)), waits=bw,
                     inc="gG3", amt=len(batch), relax=4)

        def gamma_final(wi, extra=()):
            g0, g1 = ((0, 128), (128, 256), (256, 344))[wi]
            if wi == 2:
                gw = [("gG3", 88)]
            else:
                gw = [("gG", g_cnt(g1 - 1) if g_eng(g1 - 1) == "dve" else g1 // 2),
                      ("gGa", g1 // 2)]
                gw = [("gG", g1 // 2), ("gGa", g1 // 2)]
            S.op("dve", (lambda a, b: (lambda e: e.tensor_reduce(
                lnt[:, 0:1], gcols[:, a:b], AX.X, ALU.add)))(g0, g1),
                 waits=gw + [("pGM", S.c("pGM"))] + list(extra), inc="vGS")
            vgs = S.c("vGS")
            S.op("pe", lambda e: e.matmul(ps8[:1, 7, 0:1], lnt[:, 0:1], ones_col[:],
                                          start=True, stop=True),
                 waits=[("vGS", vgs), ("aDELTA", S.c("aDELTA")),
                        ("vR1", S.c("vR1"))], inc="pGM")
            pg = S.c("pGM")
            S.op("dve", lambda e: e.tensor_copy(gtmp[:], ps8[:1, 7, 0:1]),
                 waits=[("pGM", pg)], inc="vGS")
            vgs2 = S.c("vGS")
            S.op("pe", lambda e: e.matmul(ps8[:, 7, 1:2], ones_row[:], gtmp[:],
                                          start=True, stop=True),
                 waits=[("vGS", vgs2)], inc="pGM")
            pg2 = S.c("pGM")
            S.op("dve", (lambda w: (lambda e: e.tensor_scalar(
                gscale[:, w:w + 1], ps8[:, 7, 1:2], 1.0 / NW_ELEM, EPS_Q,
                ALU.mult, ALU.add)))(wi),
                 waits=[("pGM", pg2)], inc="vGS")
            S.op("dve", (lambda w: (lambda e: e.tensor_scalar(
                gscale[:, 3 + w:4 + w], gscale[:, w:w + 1], 0.5, None,
                ALU.mult)))(wi), inc="vGS")
            S.op("dve", (lambda w: (lambda e: e.tensor_scalar(
                gscale[:, 6 + w:7 + w], gscale[:, w:w + 1], -0.5, None,
                ALU.mult)))(wi), inc="vGS")
            S.op("dve", (lambda w: (lambda e: e.reciprocal(
                gscale[:, w:w + 1], gscale[:, w:w + 1])))(wi), inc="vGS")

        S.op("act", lambda e: e.activation(a_neg[:], afl, AF.Exp),
             waits=[("dWM", n_wm)], inc="aAA")
        S.op("dve", lambda e: e.tensor_scalar_mul(a_neg[:], a_neg[:], -1.0),
             waits=[("aAA", 1)], inc="vB")

        def emit_rest_weights():
            for (dst, src, nj, sem) in ((winz, winzT, NKD, "dWZ"),
                                        (wpar, wparamT, NCT, "dWP"),
                                        (wout, woutT, NCT, "dWO")):
                src_r = src.rearrange("(j p) f -> p j f", p=128)
                step = max(1, nj // 2)
                for j0 in range(0, nj, step):
                    j1 = min(nj, j0 + step)
                    S.dma("pool", dst[:, j0:j1, :], src_r[:, j0:j1, :],
                          inc=sem)
            S.dma("pool", wdt[:], wdtT[:], inc="dWD")

        mst = {"tsub": 0}
        vx1_of = {}

        def own(g):
            return g >= 4

        def emit_ipx(g):
            ci = g % 4
            w = [("dHS%d" % ((2 * g) % 3), 32 * ((2 * g) // 3 + 1)),
                 ("dHS%d" % ((2 * g + 1) % 3), 32 * ((2 * g + 1) // 3 + 1)),
                 ("vEVX", max(0, 2 * g - 1)),
                 ("vTPC", S.c("vTPC"))]
            if g == 0:
                w += [("dWX", 64)]

            def f(e):
                ii = None
                for k in range(NKD):
                    sl = (2 * g + k // 8) % 3
                    for c in range(NCT):
                        ii = e.matmul(ps8[:, c, :], winx[:, k, c * 128:(c + 1) * 128],
                                      hsb[:, sl, k % 8, :], start=(k == 0),
                                      stop=(k == NKD - 1))
                return ii
            S.op("pe", f, waits=w, inc="pIPx")

        def emit_ipz(g):
            w = [("vR1", S.c("vR1")), ("vPAR", S.c("vPAR")),
                 ("aDELTA", S.c("aDELTA")), ("aZ", S.c("aZ")),
                 ("vGS", S.c("vGS")), ("dWZ", 32)]

            def f(e):
                ii = None
                for k in range(NKD):
                    sl = (2 * g + k // 8) % 3
                    for c in range(NCT):
                        ii = e.matmul(ps8[:, 4 + c, :],
                                      winz[:, k, c * 128:(c + 1) * 128],
                                      hsb[:, sl, k % 8, :], start=(k == 0),
                                      stop=(k == NKD - 1))
                return ii
            S.op("pe", f, waits=w, inc="pIPz")

        def emit_evacx(g):
            ci = g % 4
            cvw = [("vEVX", 2 * g)] if g > 0 else []
            if g == 0:
                S.op("dve", lambda e: e.memset(x_sb[:, :, 0:3], 0.0))
            elif g == 4:
                S.op("dve", lambda e: e.tensor_scalar(
                    x_sb[:, :, 0:3], xtail[:], fl, None, ALU.mult), waits=cvw)
            else:
                S.op("dve", lambda e: e.tensor_copy(x_sb[:, :, 0:3],
                                                    x_sb[:, :, TC:TC + 3]),
                     waits=cvw)
            S.op("dve", lambda e: e.tensor_copy(x_sb[:, :, 3:TC + 3],
                                                ps8[:, 0:NCT, :]),
                 waits=[("pIPx", g + 1)], inc="vEVX")
            if g == 3:
                S.op("dve", lambda e: e.tensor_copy(xtail[:], x_sb[:, :, TC:TC + 3]))

        def emit_conv_silu_cast(g):
            def conv_fn(e):
                ii = None
                for c in range(NCT):
                    ii = e.tensor_scalar(xact[:, c, :], x_sb[:, c, 0:TC],
                                         cw[:, 4 * c:4 * c + 1], None, ALU.mult)
                for k in range(1, 4):
                    for c in range(NCT):
                        ii = e.scalar_tensor_tensor(
                            xact[:, c, :], x_sb[:, c, k:k + TC],
                            cw[:, 4 * c + k:4 * c + k + 1], xact[:, c, :],
                            ALU.mult, ALU.add)
                return ii
            cw_w = [("vB", S.c("vB")), ("vY3", S.c("vY3"))]
            if g == 0:
                cw_w.append(("dWM", n_wm))
            S.op("dve", conv_fn, waits=cw_w, inc="vEVX", relax=4)
            vcv = S.c("vEVX")

            def silu_fn(e):
                ii = None
                for c in range(NCT):
                    ii = e.activation(xact[:, c, :], xact[:, c, :], AF.Silu,
                                      bias=cb[:, c:c + 1], scale=1.0)
                return ii
            S.op("act", silu_fn, waits=[("vEVX", vcv)], inc="aXA", relax=4)
            S.op("dve", lambda e: e.tensor_copy(xactb[:], xact[:]),
                 waits=[("aXA", S.c("aXA")), ("pPAR", S.c("pPAR"))], inc="vXB")

        def emit_siluz(g):
            S.op("act", lambda e: e.activation(sz[:], ps8[:, 4:8, :], AF.Silu),
                 waits=[("pIPz", g - 3), ("vY3", S.c("vY3"))], inc="aZ")

        def emit_params(g):
            npar = NPAR if own(g) else NPAR_AUX
            vxb = S.c("vXB")
            vp0 = S.c("vPAR")

            def par_mm(mm):
                bank = 4 + mm % 4
                def f(e):
                    ii = None
                    for c in range(NCT):
                        ii = e.matmul(ps8[:, bank, :],
                                      wpar[:, c, mm * 128:(mm + 1) * 128],
                                      xactb[:, c, :], start=(c == 0),
                                      stop=(c == NCT - 1))
                    return ii
                return f

            for m in range(npar):
                w = []
                if m == 0:
                    w = [("vXB", vxb), ("pDT", S.c("pDT")),
                         ("vPAR", S.c("vPAR")), ("vR1", S.c("vR1")),
                         ("dWP", 32)]
                    if own(g):
                        w.append(("aZ", S.c("aZ")))
                elif m == 4:
                    w = [("vPAR", vp0 + 1)]
                elif m == 5:
                    w = [("vPAR", vp0 + 2)]
                elif m == 8:
                    w = [("vPAR", vp0 + 3)]
                S.op("pe", par_mm(m), waits=w, inc="pPAR")
                pp = S.c("pPAR")
                if m == 0:
                    S.op("dve", lambda e: e.tensor_copy(dtb[:], ps8[:, 4, :]),
                         waits=[("pPAR", pp)], inc="vPAR")
                elif m == 3:
                    S.op("dve", lambda e: e.tensor_copy(bmat[:, 0:3, :],
                                                        ps8[:, 5:8, :]),
                         waits=[("pPAR", pp)], inc="vPAR")
                elif m == 4:
                    S.op("dve", lambda e: e.tensor_copy(bmat[:, 3, :],
                                                        ps8[:, 4, :]),
                         waits=[("pPAR", pp)], inc="vPAR")
                elif m == 7:
                    S.op("dve", lambda e: e.tensor_copy(cmat[:, 0:3, :],
                                                        ps8[:, 5:8, :]),
                         waits=[("pPAR", pp)], inc="vPAR")
                elif m == 8:
                    S.op("dve", lambda e: e.tensor_copy(cmat[:, 3, :],
                                                        ps8[:, 4, :]),
                         waits=[("pPAR", pp)], inc="vPAR")

        def emit_dt(g):
            vpar = S.c("vPAR")
            for c in range(NCT):
                w = []
                if c == 0:
                    w = [("vPAR", vpar), ("dWD", 16)]
                S.op("pe", (lambda bk, cc: (lambda e: e.matmul(
                    ps8[:, bk, :], wdt[:, cc * 128:(cc + 1) * 128], dtb[:],
                    start=True, stop=True)))(4 + c, c),
                    waits=w, inc="pDT")
            pd = S.c("pDT")

            def exp_fn(e):
                ii = None
                for c in range(NCT):
                    ii = e.activation(delta[:, c, :], ps8[:, 4 + c, :], AF.Exp,
                                      bias=bd[:, c:c + 1], scale=1.0)
                return ii
            S.op("act", exp_fn, waits=[("pDT", pd), ("vB", S.c("vB"))],
                 relax=4)

            def ln_fn(e):
                ii = None
                for c in range(NCT):
                    ii = e.activation(delta[:, c, :], delta[:, c, :], AF.Ln,
                                      bias=consts[:, 2:3], scale=1.0)
                return ii
            S.op("act", ln_fn, inc="aDELTA", relax=4)

        def emit_scan(g):
            adel = S.c("aDELTA")
            state = state_own if own(g) else state_aux

            def ab_wide(e):
                e.tensor_tensor(bmat[:], delta[:], bmat[:], ALU.mult)
                return e.tensor_tensor(bmat[:], bmat[:], xact[:], ALU.mult)
            S.op("dve", ab_wide, waits=[("aDELTA", adel)], inc="vB")

            def ab_dn(e):
                ii = None
                for c in range(NCT):
                    ii = e.tensor_scalar(delta[:, c, :], delta[:, c, :],
                                         a_neg[:, c:c + 1], None, ALU.mult)
                return ii
            S.op("dve", ab_dn, inc="vB", relax=4)

            def a_fn(e):
                ii = None
                for c in range(NCT):
                    ii = e.activation(a_sb[:, c, :], delta[:, c, :], AF.Exp)
                return ii
            S.op("act", a_fn, waits=[("vB", S.c("vB")), ("vSCAN", S.c("vSCAN"))],
                 inc="aAA", relax=4)

            if g == 4:
                S.op("dve", lambda e: e.tensor_scalar(
                    state_own[:], state_aux[:], fl, None, ALU.mult))

            zero_init = (g == 0)

            def scan_fn(e):
                for c in range(NCT):
                    init = 0.0 if zero_init else state[:, c:c + 1]
                    e.tensor_tensor_scan(h_sb[:, c, :], a_sb[:, c, :],
                                         bmat[:, c, :], init, ALU.mult,
                                         ALU.add)
                ii = None
                for c in range(NCT):
                    ii = e.tensor_copy(state[:, c:c + 1], h_sb[:, c, TC - 1:TC])
                return ii
            S.op("dve", scan_fn, waits=[("aAA", S.c("aAA"))], inc="vSCAN",
                 relax=4)

        def emit_y3(g):
            S.op("dve", lambda e: e.tensor_tensor(cmat[:], cmat[:], h_sb[:],
                                                  ALU.mult),
                 waits=[("vSCAN", S.c("vSCAN")), ("aZ", S.c("aZ")),
                        ("pWOUT", S.c("pWOUT"))])

            def y_a(e):
                ii = None
                for c in range(NCT):
                    ii = e.scalar_tensor_tensor(cmat[:, c, :], xact[:, c, :],
                                                dv[:, c:c + 1], cmat[:, c, :],
                                                ALU.mult, ALU.add)
                return ii
            S.op("dve", y_a, relax=4)
            S.op("dve", lambda e: e.tensor_tensor(y3[:], cmat[:], sz[:],
                                                  ALU.mult), inc="vY3")

        def emit_wout_tail(g):
            ci = g - 4
            vy3 = S.c("vY3")
            for pair in range(2):
                for s in (2 * pair, 2 * pair + 1):
                    tt = 4 * ci + s
                    trow = ci * TC + s * 128
                    S.dma("act", r1[:, tt % 2, :], hs_tm[trow:trow + 128, :],
                          waits=[("wXT%d" % (tt % 2),
                                  16 * max(0, tt // 2))], inc="dHSTM%d" % (tt % 2))
                    for dh in range(2):
                        w = [("vY3", vy3)]
                        w.append(("vR1", max(0, S.c("vR1") - 1)))
                        if s == 0 and dh == 0:
                            w += [("vPAR", S.c("vPAR")),
                                  ("aDELTA", S.c("aDELTA")),
                                  ("vGS", S.c("vGS")), ("dWO", 32)]

                        def wo_fn(ss, dd):
                            def f(e):
                                ii = None
                                for n in range(2):
                                    col = 2 * dd + n
                                    for c in range(NCT):
                                        ii = e.matmul(
                                            ps8[:, 4 + 2 * dd + n, :],
                                            y3[:, c, ss * 128:(ss + 1) * 128],
                                            wout[:, c, col * 512:(col + 1) * 512],
                                            start=(c == 0), stop=(c == NCT - 1))
                                return ii
                            return f
                        S.op("pe", wo_fn(s, dh), waits=w, inc="pWOUT")
                        pw = S.c("pWOUT")
                        S.op("dve", (lambda ttt, dd: (lambda e: e.scalar_tensor_tensor(
                            r1[:, ttt % 2, dd * 1024:(dd + 1) * 1024],
                            ps8[:, 4 + 2 * dd:6 + 2 * dd, :].rearrange("p a b -> p (a b)"),
                            1.0, r1[:, ttt % 2, dd * 1024:(dd + 1) * 1024],
                            ALU.mult, ALU.add,
                            accum_out=parts[:, dd:dd + 1])))(tt, dh),
                            waits=[("pWOUT", pw),
                                   ("dHSTM%d" % (tt % 2), 16 * (tt // 2 + 1))],
                            inc="vR1")
                    S.op("dve", (lambda ttt: (lambda e: e.tensor_tensor(
                        sums[:, ttt:ttt + 1], parts[:, 0:1], parts[:, 1:2],
                        ALU.add)))(tt))
                    S.op("act", (lambda ttt: (lambda e: e.activation(
                        junk2k[:], r1[:, ttt % 2, :], AF.Square,
                        accum_out=sumsq[:, ttt:ttt + 1])))(tt),
                        waits=[("vR1", S.c("vR1"))], inc="aSQ")

                t0 = 4 * ci + 2 * pair
                asq = S.c("aSQ")
                S.op("dve", (lambda tz: (lambda e: e.tensor_scalar(
                    negm[:, tz:tz + 2], sums[:, tz:tz + 2], -1.0 / D, None,
                    ALU.mult)))(t0),
                    waits=[("aSQ", asq)])
                S.op("dve", (lambda tz: (lambda e: e.tensor_tensor(
                    lnv[:, 0:2], negm[:, tz:tz + 2], negm[:, tz:tz + 2],
                    ALU.mult)))(t0))
                S.op("dve", (lambda tz: (lambda e: e.scalar_tensor_tensor(
                    lnv[:, 0:2], sumsq[:, tz:tz + 2], 1.0 / D, lnv[:, 0:2],
                    ALU.mult, ALU.subtract)))(t0), inc="vNRM")
                vn = S.c("vNRM")
                S.op("act", lambda e: e.activation(lnv[:, 0:2], lnv[:, 0:2],
                                                   AF.Ln, bias=consts[:, 3:4],
                                                   scale=1.0),
                     waits=[("vNRM", vn)])
                S.op("act", (lambda tz: (lambda e: e.activation(
                    rstd[:, tz:tz + 2], lnv[:, 0:2], AF.Exp, scale=-0.5)))(t0),
                    inc="aRSTD")
                S.op("dve", (lambda tz: (lambda e: e.tensor_scalar(
                    rstdh[:, tz:tz + 2], rstd[:, tz:tz + 2], 0.5, None,
                    ALU.mult)))(t0),
                    waits=[("aRSTD", S.c("aRSTD"))])

                for s in (2 * pair, 2 * pair + 1):
                    tt = 4 * ci + s
                    trow = ci * TC + s * 128
                    S.op("dve", (lambda ttt: (lambda e: e.tensor_scalar(
                        x1bf[:, ttt % 4, :], r1[:, ttt % 2, :],
                        negm[:, ttt:ttt + 1], rstdh[:, ttt:ttt + 1],
                        ALU.add, ALU.mult)))(tt),
                        waits=[("pTP", max(0, tt - 3))], inc="vX1")
                    vx1_of[tt] = S.c("vX1")
                    S.op("dve", (lambda ttt: (lambda e: e.tensor_scalar(
                        r1[:, ttt % 2, :], r1[:, ttt % 2, :],
                        negm[:, ttt:ttt + 1], rstd[:, ttt:ttt + 1],
                        ALU.add, ALU.mult)))(tt), inc="vX1")
                    vx1 = S.c("vX1")
                    S.dma("act", x1tm_sp[trow:trow + 128, :], r1[:, tt % 2, :],
                          waits=[("vX1", vx1)], inc="wXT%d" % (tt % 2))

        def emit_tp(j, vevx_need):
            ci = j - 4
            for s in range(4):
                tt = 4 * ci + s
                trow = ci * TC + s * 128
                w = [("vX1", vx1_of[tt]), ("vTPC", tt)]
                if s == 0:
                    w.append(("vEVX", vevx_need))

                def tp_fn(ttt):
                    def f(e):
                        ii = None
                        for jj in range(NKD):
                            ii = e.transpose(
                                ptp[:, jj // 8, (jj % 8) * 128:(jj % 8 + 1) * 128],
                                x1bf[:, ttt % 4, jj * 128:(jj + 1) * 128], ident[:])
                        return ii
                    return f
                S.op("pe", tp_fn(tt), waits=w, inc="pTP")
                S.op("dve", (lambda ttt: (lambda e: e.tensor_copy(
                    tpst[:], ptp.rearrange("p a b -> p (a b)"))))(tt),
                    waits=[("pTP", S.c("pTP")), ("wXC", 32 * tt)],
                    inc="vTPC")
                vtpc = S.c("vTPC")
                tps3 = tpst.rearrange("p (jj t) -> p jj t", t=128)
                for jh in range(2):
                    S.dma("act",
                          x1cm_sp.rearrange("j p t -> p j t")[
                              :, jh * 8:(jh + 1) * 8, trow:trow + 128],
                          tps3[:, jh * 8:(jh + 1) * 8, :],
                          waits=([("vTPC", vtpc)] if jh == 0 else []),
                          inc="wXC")

        emit_hs_half(0)
        emit_hs_half(1)
        emit_hs_half(2)
        emit_rest_weights()

        gamma_dma(4)

        def chunk_vector_chain(g):
            emit_conv_silu_cast(g)
            if own(g):
                emit_siluz(g)
            emit_params(g)
            emit_dt(g)
            emit_scan(g)
            if own(g):
                emit_y3(g)

        if STAGE >= 1:
            emit_ipx(0)
            emit_hs_half(3)
            emit_ipx(1)
            for g in range(4):
                emit_evacx(g)
                chunk_vector_chain(g)
                gamma_reduce(40)
                gamma_dma(40)
                if g + 2 <= 3:
                    emit_hs_half(2 * (g + 2))
                    emit_hs_half(2 * (g + 2) + 1)
                    emit_ipx(g + 2)
                elif g + 2 == 4:
                    emit_hs_half(8)
                    emit_hs_half(9)
                    emit_ipx(4)

        KSUB = os.environ.get("KSUB", "e")
        DO_WOUT = KSUB >= "b"
        DO_TP = KSUB >= "d"
        DO_FIN = KSUB >= "e"
        if STAGE >= 2:
            for g in range(4, 8):
                emit_evacx(g)
                if g >= 6 and DO_TP:
                    emit_tp(g - 2, 2 * g + 1)
                if g > 4 and DO_WOUT:
                    emit_wout_tail(g - 1)
                emit_ipz(g)
                chunk_vector_chain(g)
                gamma_reduce(24)
                gamma_dma(24)
                if g + 1 <= 7:
                    emit_hs_half(2 * (g + 1))
                    emit_hs_half(2 * (g + 1) + 1)
                    emit_ipx(g + 1)
            if DO_TP:
                emit_tp(6, 15)
            if DO_WOUT:
                emit_wout_tail(7)
            if DO_TP:
                emit_tp(7, 15)
            gamma_dma(256)
            gamma_reduce(256)
            if DO_FIN:
                gamma_final(0)
                gamma_final(1)
        else:
            gamma_dma(256)
            gamma_reduce(256)
            gamma_final(0)
            gamma_final(1)

        BAR = [("pGM", S.c("pGM")), ("vGS", S.c("vGS")),
               ("aRSTD", S.c("aRSTD")), ("pTP", S.c("pTP")), ("vTPC", S.c("vTPC")),
               ("wXT0", S.c("wXT0")), ("wXT1", S.c("wXT1")), ("wXC", S.c("wXC")),
               ("vR1", S.c("vR1")), ("vX1", S.c("vX1")),
               ("vY3", S.c("vY3")), ("vSCAN", S.c("vSCAN")),
               ("pWOUT", S.c("pWOUT")), ("pPAR", S.c("pPAR")),
               ("pDT", S.c("pDT")), ("pIPx", S.c("pIPx")),
               ("pIPz", S.c("pIPz")), ("aZ", S.c("aZ")),
               ("aXA", S.c("aXA")), ("vXB", S.c("vXB")),
               ("aSQ", S.c("aSQ")), ("gG", S.c("gG")), ("gGa", S.c("gGa")),
               ("dWM", n_wm), ("dWX", 64), ("dWZ", 32), ("dWP", 32),
               ("dWO", 32), ("dWD", 16), ("init", 1)]

        w12src = {"w1": w1T, "w2": w2T}
        qsdst = {"w1": q1s, "w2": q2s}
        gsi = {"w1": 0, "w2": 1}
        wQ_A = {}
        fst = {"x1tmld": 0, "r2ev": 0}
        asq_of = {}

        def ffn_half(h, sub=2):
            for j in range(NKD):
                w = []
                if j == 0:
                    w = list(BAR) if h == 0 else [("pA", 88)]
                S.dma("sp", x1cm[:, j, :], x1cm_sp[j, :, h * TH:(h + 1) * TH],
                      waits=w, inc="dX1CM")
            dx1 = S.c("dX1CM")
            if sub < 1:
                return

            gidx = [0]
            vq_of = {}
            dw_of = {}
            sig_of = {}
            hid_of = {}
            pa_of = {}

            def a_quant(win):
                for wname in ("w1", "w2"):
                    for g in range(2):
                        qdst = qb[wname][win % 2][:, g * 8:(g + 1) * 8, :]
                        if h == 0:
                            src = w12src[wname].rearrange("(j p) f -> p j f", p=128)
                            stg = wstage[gidx[0] % 4]
                            gidx[0] += 1
                            wst = [("aSGN", max(0, S.c("aSGN") - 3))]
                            if win == 0 and wname == "w1" and g == 0:
                                wst = wst + BAR
                            slot = "dW12_%s%d%d" % (wname[1], g, win % 2)
                            S.dma("sp", stg[:],
                                  src[:, g * 8:(g + 1) * 8,
                                      win * FWIN:(win + 1) * FWIN],
                                  waits=wst, inc=slot)
                            qwar = [(slot, S.c(slot))]
                            if win >= 2:
                                qwar += [("pA", 2 * (win - 1)),
                                         ("wQ12_%d" % (win % 2), 64 * (win // 2))]
                            wi = gsi[wname]
                            if False:
                                sc = gscale[:, wi:wi + 1]
                                S.op("act", (lambda st, qd, s2: (lambda e: e.activation(
                                    qd, st[:], AF.Sign, bias=consts[:, 0:1],
                                    scale=s2)))(stg, qdst, sc),
                                    waits=qwar)
                                S.op("act", (lambda st, s2: (lambda e: e.activation(
                                    sgnbA[:], st[:], AF.Sign, bias=consts[:, 1:2],
                                    scale=s2)))(stg, sc),
                                    waits=[("vQADD", S.c("vQADD"))], inc="aSGN")
                                S.op("dve", (lambda qd: (lambda e: e.tensor_tensor(
                                    qd, qd, sgnbA[:], ALU.add)))(qdst),
                                    waits=[("aSGN", S.c("aSGN"))], inc="vQADD")
                            else:
                                def cmp_fn(st, w2):
                                    def f(e):
                                        e.tensor_scalar(
                                            tmpq[:, 0, :, :], st,
                                            gscale[:, 3 + w2:4 + w2],
                                            2.0, ALU.is_gt, ALU.mult)
                                        return e.tensor_scalar(
                                            tmpq[:, 1, :, :], st,
                                            gscale[:, 6 + w2:7 + w2],
                                            2.0, ALU.is_lt, ALU.mult)
                                    return f
                                S.op("dve", cmp_fn(stg[:], wi), waits=qwar,
                                     inc="aSGN", relax=2)
                                S.op("dve", (lambda qd: (lambda e: e.tensor_tensor(
                                    qd, tmpq[:, 0, :, :], tmpq[:, 1, :, :],
                                    ALU.subtract)))(qdst),
                                    inc="vQADD")
                            vq = S.c("vQADD")
                            S.dma("pool",
                                  qsdst[wname].rearrange("(j p) f -> p j f", p=128)[
                                      :, g * 8:(g + 1) * 8,
                                      win * FWIN:(win + 1) * FWIN],
                                  qdst, waits=[("vQADD", vq)],
                                  inc="wQ12_%d" % (win % 2))
                        else:
                            w = [("pA", 86 + 2 * win)]
                            if win == 0 and wname == "w1" and g == 0:
                                w.append(("wQ12_0", wQ_A["t0"]))
                                w.append(("wQ12_1", wQ_A["t1"]))
                            slot = "dW12_%s%d%d" % (wname[1], g, win % 2)
                            S.dma("sp", qdst,
                                  qsdst[wname].rearrange("(j p) f -> p j f", p=128)[
                                      :, g * 8:(g + 1) * 8,
                                      win * FWIN:(win + 1) * FWIN],
                                  waits=w, inc=slot)
                vq_of[win] = S.c("vQADD")
                dw_of[win] = [("dW12_%s%d%d" % (wn, gg, win % 2),
                               S.c("dW12_%s%d%d" % (wn, gg, win % 2)))
                              for wn in ("1", "2") for gg in range(2)]
                if h == 0 and win % 2 == 1:
                    gamma3_batch(4, g3slots)

            def a_mm_step(fc):
                par = fc % 2
                if h == 0:
                    wmm = [("vQADD", vq_of[fc]), ("dX1CM", dx1)]
                else:
                    wmm = [("dX1CM", dx1)] + dw_of[fc]
                if fc >= 2:
                    wmm += [("aSIG", sig_of[fc - 2]), ("vHID", hid_of[fc - 2])]
                elif h == 1:
                    wmm += [("aSIG", wQ_A["sig"][fc]), ("vHID", wQ_A["hid"][fc]),
                            ("vR2S", wQ_A["r2ev"])]

                def a_mm(wname_i, bank0, fci):
                    qsb = qb[wname_i][fci % 2]
                    def f(e):
                        ii = None
                        for n in range(2):
                            for k in range(NKD):
                                ii = e.matmul(ps8[:, bank0 + n, :],
                                              qsb[:, k, 0:128],
                                              x1cm[:, k, n * 512:(n + 1) * 512],
                                              start=(k == 0),
                                              stop=(k == NKD - 1))
                        return ii
                    return f
                b0 = 0 if par == 0 else 4
                S.op("pe", a_mm("w1", b0, fc), waits=wmm, inc="pA")
                S.op("pe", a_mm("w2", b0 + 2, fc), inc="pA")
                pa_of[fc] = S.c("pA")

            def a_evac(fc):
                b0 = 0 if fc % 2 == 0 else 4
                S.op("act", (lambda bk, ib: (lambda e: e.activation(
                    sigb[ib].rearrange("p (a b) -> p a b", a=2),
                    ps8[:, bk:bk + 2, :], AF.Sigmoid, scale=1.0)))(b0, fc % 2),
                    waits=[("pA", pa_of[fc])], inc="aSIG")
                sig_of[fc] = S.c("aSIG")
                hwait = [("aSIG", sig_of[fc])]
                if h == 1 and fc == 0:
                    hwait.append(("pBf", 88))
                S.op("dve", (lambda bk, fci, ib: (lambda e: e.scalar_tensor_tensor(
                    hidden[:, fci, :].rearrange("p (a b) -> p a b", a=2),
                    ps8[:, bk + 2:bk + 4, :], 0.5,
                    sigb[ib].rearrange("p (a b) -> p a b", a=2),
                    ALU.mult, ALU.mult)))(b0, fc, fc % 2),
                    waits=hwait, inc="vHID")
                hid_of[fc] = S.c("vHID")

            for win in range(NWIN + 2):
                if win < NWIN:
                    a_quant(win)
                if win >= 1 and win - 1 < NWIN:
                    a_mm_step(win - 1)
                if win >= 2:
                    a_evac(win - 2)
            if h == 0:
                wQ_A["t0"] = S.c("wQ12_0")
                wQ_A["t1"] = S.c("wQ12_1")
                wQ_A["sig"] = (sig_of[NWIN - 2], sig_of[NWIN - 1])
                wQ_A["hid"] = (hid_of[NWIN - 2], hid_of[NWIN - 1])
                gamma3_batch(88, g3slots)
                gamma_final(2, extra=[("vHID", S.c("vHID")),
                                      ("aSIG", S.c("aSIG"))])

            if sub < 2:
                return

            vhid_all = S.c("vHID")
            for dg in range(4):
                for grp in range(22):
                    if h == 0:
                        src = w3T.rearrange("(q p) d -> p q d", p=128)
                        stg = w3stage[grp % 2]
                        slot3 = "dW3s%d" % (grp % 2)
                        S.dma("sp", stg[:],
                              src[:, grp * 2:(grp + 1) * 2,
                                  dg * 512:(dg + 1) * 512],
                              waits=[("aSGN", max(0, S.c("aSGN") - 1))], inc=slot3)
                        sc = gscale[:, 2:3]
                        qdst = q3b[grp % 3]
                        sgn_w = [(slot3, S.c(slot3)),
                                 ("pBf", max(0, S.c("pBf") - 2)),
                                 ("wQ3_%d" % (grp % 3),
                                  S.c("wQ3_%d" % (grp % 3)))]
                        if dg == 0 and grp == 0:
                            sgn_w.append(("vGS", S.c("vGS")))
                        tq2 = tmpq.rearrange("p a j b -> p a (j b)").rearrange(
                            "p a (q b) -> p a q b", q=2)
                        S.op("dve", (lambda st: (lambda e: e.tensor_scalar(
                            tq2[:, 0, :, :], st[:], gscale[:, 5:6],
                            2.0, ALU.is_gt, ALU.mult)))(stg),
                            waits=sgn_w)
                        S.op("dve", (lambda st: (lambda e: e.tensor_scalar(
                            tq2[:, 1, :, :], st[:], gscale[:, 8:9],
                            2.0, ALU.is_lt, ALU.mult)))(stg),
                            inc="aSGN")
                        S.op("dve", (lambda qd: (lambda e: e.tensor_tensor(
                            qd[:], tq2[:, 0, :, :], tq2[:, 1, :, :],
                            ALU.subtract)))(qdst),
                            inc="vQADD")
                        vq = S.c("vQADD")
                        S.dma("pool",
                              q3s.rearrange("(q p) d -> p q d", p=128)[
                                  :, grp * 2:(grp + 1) * 2,
                                  dg * 512:(dg + 1) * 512],
                              qdst[:], waits=[("vQADD", vq)], inc="wQ3_%d" % (grp % 3))
                    else:
                        wq3 = [("pBf", max(0, S.c("pBf") - 2))]
                        if dg == 0 and grp == 0:
                            for qsem in ("wQ3_0", "wQ3_1", "wQ3_2"):
                                wq3.append((qsem, S.c(qsem)))
                        slotq = "dW3q%d" % (grp % 3)
                        S.dma("sp", q3b[grp % 3][:],
                              q3s.rearrange("(q p) d -> p q d", p=128)[
                                  :, grp * 2:(grp + 1) * 2,
                                  dg * 512:(dg + 1) * 512],
                              waits=wq3, inc=slotq)
                    if h == 0:
                        wmm = [("vQADD", S.c("vQADD"))]
                    else:
                        wmm = [(slotq, S.c(slotq))]
                    if grp == 0 and dg == 0:
                        wmm.append(("vHID", vhid_all))
                    if grp == 0 and dg > 0:
                        wmm.append(("vR2S", S.c("vR2S")))

                    def b_mm(grpi):
                        def f(e):
                            ii = None
                            for q in range(2):
                                ft = grpi * 2 + q
                                for s in range(8):
                                    ii = e.matmul(ps8[:, s, :],
                                                  hidden[:, ft, s * 128:(s + 1) * 128],
                                                  q3b[grpi % 3][:, q, :],
                                                  start=(ft == 0),
                                                  stop=(ft == NFC - 1))
                            return ii
                        return f
                    S.op("pe", b_mm(grp), waits=wmm, inc="pBf")
                pb = S.c("pBf")

                for s in range(8):
                    tt = h * 8 + s
                    trow = h * TH + s * 128
                    li = fst["x1tmld"]
                    S.dma("sp", x1tmld[li % 2][:],
                          x1tm_sp[trow:trow + 128, dg * 512:(dg + 1) * 512],
                          waits=[("wXT0", 128), ("wXT1", 128),
                                 ("vR2S", max(0, li - 1))],
                          inc="dXT%d" % (li % 2))
                    fst["x1tmld"] += 1
                    ri = fst["r2ev"]
                    fst["r2ev"] += 1
                    evw = [("pBf", pb),
                           ("dXT%d" % (li % 2), S.c("dXT%d" % (li % 2))),
                           ("wR2_%d" % (ri % 2), max(0, 16 * (ri // 2)))]
                    if ri >= 2:
                        evw.append(("aSQ", asq_of[ri - 2]))
                    S.op("dve", (lambda ss, lii, rii: (lambda e: e.scalar_tensor_tensor(
                        r2t[rii % 2][:], ps8[:, ss, :], 1.0, x1tmld[lii % 2][:],
                        ALU.mult, ALU.add, accum_out=partsB[:, ss:ss + 1])))(s, li, ri),
                        waits=evw, inc="vR2S")
                    ev = S.c("vR2S")
                    S.dma("pool", r2_sp[trow:trow + 128, dg * 512:(dg + 1) * 512],
                          r2t[ri % 2][:], waits=[("vR2S", ev)],
                          inc="wR2_%d" % (ri % 2))
                    S.op("act", (lambda rii, ss: (lambda e: e.activation(
                        junk512[:], r2t[rii % 2][:], AF.Square,
                        accum_out=partsB[:, 8 + ss:9 + ss])))(ri, s),
                        waits=[("vR2S", ev)], inc="aSQ")
                    asq_of[ri] = S.c("aSQ")

                t0c = h * 8
                fw = [("aSQ", asq_of[fst["r2ev"] - 1])]
                if dg == 0:
                    S.op("dve", (lambda tz: (lambda e: e.tensor_copy(
                        sums[:, tz:tz + 8], partsB[:, 0:8])))(t0c), waits=fw)
                    S.op("dve", (lambda tz: (lambda e: e.tensor_copy(
                        sumsq[:, tz:tz + 8], partsB[:, 8:16])))(t0c),
                        inc="vR2EV")
                else:
                    S.op("dve", (lambda tz: (lambda e: e.tensor_tensor(
                        sums[:, tz:tz + 8], sums[:, tz:tz + 8],
                        partsB[:, 0:8], ALU.add)))(t0c), waits=fw)
                    S.op("dve", (lambda tz: (lambda e: e.tensor_tensor(
                        sumsq[:, tz:tz + 8], sumsq[:, tz:tz + 8],
                        partsB[:, 8:16], ALU.add)))(t0c),
                        inc="vR2EV")

                if dg == 3:
                    vre = S.c("vR2EV")
                    S.op("dve", (lambda tz: (lambda e: e.tensor_scalar(
                        negm[:, tz:tz + 8], sums[:, tz:tz + 8], -1.0 / D,
                        None, ALU.mult)))(t0c),
                        waits=[("vR2EV", vre), ("aRSTD", S.c("aRSTD"))])
                    S.op("dve", (lambda tz: (lambda e: e.tensor_tensor(
                        lnv[:], negm[:, tz:tz + 8], negm[:, tz:tz + 8],
                        ALU.mult)))(t0c))
                    S.op("dve", (lambda tz: (lambda e: e.scalar_tensor_tensor(
                        lnv[:], sumsq[:, tz:tz + 8], 1.0 / D, lnv[:],
                        ALU.mult, ALU.subtract)))(t0c), inc="vNRM")
                    vn = S.c("vNRM")
                    S.op("act", lambda e: e.activation(
                        lnv[:], lnv[:], AF.Ln, bias=consts[:, 3:4], scale=1.0),
                        waits=[("vNRM", vn)])
                    S.op("act", (lambda tz: (lambda e: e.activation(
                        rstd[:, tz:tz + 8], lnv[:], AF.Exp, scale=-0.5)))(t0c),
                        inc="aRSTD")
                    ars = S.c("aRSTD")
                    for s in range(8):
                        tt = h * 8 + s
                        trow = h * TH + s * 128
                        S.dma("pool", r2full[0][:], r2_sp[trow:trow + 128, :],
                              waits=[("wR2_0", S.c("wR2_0")),
                                     ("wR2_1", S.c("wR2_1")),
                                     ("gG3", 88),
                                     ("vNRM", max(0, S.c("vNRM") - 1))],
                              inc="dR2RD")
                        drr = S.c("dR2RD")
                        S.op("dve", (lambda ttt: (lambda e: e.tensor_scalar(
                            outst[0][:], r2full[0][:], negm[:, ttt:ttt + 1],
                            rstd[:, ttt:ttt + 1], ALU.add, ALU.mult)))(tt),
                            waits=[("aRSTD", ars), ("dR2RD", drr),
                                   ("dOUT", max(0, 16 * tt))], inc="vNRM")
                        vn2 = S.c("vNRM")
                        S.dma("pool", out_d[trow:trow + 128, :], outst[0][:],
                              waits=[("vNRM", vn2)], inc="dOUT")

        if STAGE >= 3:
            ffn_half(0, sub=(0 if STAGE == 30 else (1 if STAGE == 31 else 2)))
        wQ_A["r2ev"] = S.c("vR2S")
        if STAGE >= 4:
            ffn_half(1, sub=(0 if STAGE == 41 else (1 if STAGE == 42 else (3 if STAGE == 43 else 2))))
        if STAGE != 4 and STAGE < 99:
            wtail = [(n, S.c(n)) for n in (
                "vGS", "wXT0", "wXT1", "wXC", "pTP", "vTPC", "wQ12_0",
                "wQ12_1", "wQ3_0", "wQ3_1", "wQ3_2", "dHS0", "dHS1", "dHS2",
                "dHSTM0", "dHSTM1", "vSCAN",
                "aSIG", "vHID", "wR2_0", "wR2_1", "dOUT", "gG", "gGa",
                "pIPx", "pIPz", "pWOUT")]
            S.dma("sp", out_d[0:128, :], outst[0][:], waits=wtail, inc="dOUT")

        nout = S.c("dOUT")
        S.op("act", lambda e: e.nop(), waits=[("dOUT", nout)])
        S.op("sp", lambda e: e.nop(), waits=[("dOUT", nout)])
        S.op("dve", lambda e: e.nop(), waits=[("dOUT", nout)])

        self_sems = {"dve": ctx.enter_context(nc.semaphore("selfDVE")),
                     "act": ctx.enter_context(nc.semaphore("selfACT")),
                     "pool": ctx.enter_context(nc.semaphore("selfPOOL"))}
        block = ctx.enter_context(nc.Block())
        S.emit(block, self_sems)

    build_program.last_sched = S
    return nc


_NC_CACHE = {}


def kernel(L0_attn_hidden_states, L0_attn_W_in_x, L0_attn_W_in_z, L0_attn_conv_w,
           L0_attn_conv_b, L0_attn_W_param, L0_attn_W_dt, L0_attn_b_dt,
           L0_attn_A_log, L0_attn_D, L0_attn_W_out, L0_ffn_w1, L0_ffn_w2, L0_ffn_w3):
    f32 = np.float32
    hs = np.ascontiguousarray(np.asarray(L0_attn_hidden_states, dtype=f32))
    c = lambda a: np.ascontiguousarray(np.asarray(a, dtype=f32))

    winxT = c(np.asarray(L0_attn_W_in_x, dtype=f32).T)
    winzT = c(np.asarray(L0_attn_W_in_z, dtype=f32).T)
    wparamT = c(np.asarray(L0_attn_W_param, dtype=f32).T)
    wdtT = c(np.repeat(np.asarray(L0_attn_W_dt, dtype=f32), 16, axis=0).T)
    woutT = c(np.asarray(L0_attn_W_out, dtype=f32).T)
    w1T = c(np.asarray(L0_ffn_w1, dtype=f32).T)
    w2T = c(np.asarray(L0_ffn_w2, dtype=f32).T)
    w3T = c(np.asarray(L0_ffn_w3, dtype=f32).T)
    convw = c(np.asarray(L0_attn_conv_w, dtype=f32).reshape(4, 128, 4)
              .transpose(1, 0, 2).reshape(128, 16))
    convb = c(np.asarray(L0_attn_conv_b, dtype=f32).reshape(4, 128).T)
    bdt = c(np.repeat(np.asarray(L0_attn_b_dt, dtype=f32), 16).reshape(4, 128).T)
    aflat = c(np.asarray(L0_attn_A_log, dtype=f32).reshape(4, 128).T)
    dvec = c(np.asarray(L0_attn_D, dtype=f32).reshape(4, 128).T)

    if "nc" not in _NC_CACHE:
        _NC_CACHE["nc"] = build_program()
    nc = _NC_CACHE["nc"]
    in_maps = []
    for core in range(8):
        b, h = core // 2, core % 2
        hs_tm_c = np.ascontiguousarray(hs[b, h * T:(h + 1) * T, :])
        in_maps.append({
            "hs_tm": hs_tm_c,
            "hs_cm": np.ascontiguousarray(hs_tm_c.T),
            "aux_cm": np.ascontiguousarray(hs[b, 0:T, :].T),
            "winxT": winxT, "winzT": winzT, "wparamT": wparamT, "wdtT": wdtT,
            "woutT": woutT, "w1T": w1T, "w2T": w2T, "w3T": w3T,
            "convw": convw, "convb": convb, "bdt": bdt, "aflat": aflat,
            "dvec": dvec,
            "flag": np.full((128, 1), float(h), dtype=f32),
        })
    res = run_bass_kernel_spmd(nc, in_maps, core_ids=list(range(8)))
    out = np.empty((4, 4096, D), dtype=f32)
    for core in range(8):
        b, h = core // 2, core % 2
        out[b, h * T:(h + 1) * T, :] = np.asarray(res.results[core]["out"])
    return out


if __name__ == "__main__":
    import jax
    _cpu = jax.devices("cpu")[0]
    with jax.default_device(_cpu):
        import reference as _r
        ins = {k: np.asarray(v) for k, v in _r.setup_inputs(0).items()}
    got = kernel(**ins)
    with jax.default_device(_cpu):
        want = np.asarray(_r.reference(**{k: jax.device_put(v, _cpu) for k, v in ins.items()}))
    num = float(np.mean((got - want) ** 2))
    den = float(np.mean(want ** 2))
    print("resid var ratio:", num / den)
